# version 08
# baseline (speedup 1.0000x reference)
import contextlib
import numpy as np

import concourse.bass as bass
import concourse.mybir as mybir
from concourse.bass_utils import run_bass_kernel_spmd

F32 = mybir.dt.float32
F16 = mybir.dt.float16
I8 = mybir.dt.int8
AF = mybir.ActivationFunctionType
ALU = mybir.AluOpType
AX = mybir.AxisListType.X

KC = 16
S = 2048
EPS = 1e-5
DENOM = float(np.sqrt(np.float32(2048.0)) / np.float32(2048.0))
IC = 11
NPE = 12

MODC = {"sa": 0, "ca": 16, "ga": 32, "sb": 48, "cb": 64, "gb": 80}


class Buf:

    def __init__(self, t=None, name=""):
        self.t = t
        self.name = name
        self.writer = None
        self.readers = []
        self.pending = []


class PK:
    NDMA = 24
    DPOOL = {"gp": list(range(16)), "sy": list(range(16, 24))}

    def __init__(self, nc, stack):
        self.nc = nc
        self.sem = {}
        for name in ("pe", "dve", "act"):
            self.sem[name] = stack.enter_context(nc.semaphore(name))
        self.dsem = [stack.enter_context(nc.semaphore(f"dm{i}"))
                     for i in range(self.NDMA)]
        self.cnt = {"pe": 0, "dve": 0, "act": 0}
        self.dcnt = [0] * self.NDMA
        self.drot_q = {}
        self.waited = {}
        self.eng = {"pe": nc.tensor, "dve": nc.vector, "act": nc.scalar,
                    "gp": nc.gpsimd, "sy": nc.sync}

    def _wait1(self, waiter, mark):
        if mark is None:
            return
        if mark[0] == "d":
            _, idx, val = mark
            key = (waiter, "d", idx)
            sem = self.dsem[idx]
        else:
            name, val = mark
            key = (waiter, name)
            sem = self.sem[name]
        if self.waited.get(key, 0) >= val:
            return
        self.eng[waiter].wait_ge(sem, val)
        self.waited[key] = val

    def waits(self, waiter, marks):
        for m in marks:
            self._wait1(waiter, m)

    def _inc(self, inst, name):
        self.cnt[name] += 1
        inst.then_inc(self.sem[name], 1)
        return (name, self.cnt[name])

    def _pre(self, waiter, reads, writes, marks):
        for b in reads:
            self._wait1(waiter, b.writer)
        for w in writes:
            self._wait1(waiter, w.writer)
            self.waits(waiter, w.readers)
        self.waits(waiter, marks)

    def _post(self, m, reads, writes):
        for b in reads:
            b.readers.append(m)
        for w in writes:
            w.writer = m
            w.readers = []

    def v(self, fn, *args, marks=(), reads=(), writes=(), **kw):
        self._wait1("dve", ("dve", self.cnt["dve"]))
        self._pre("dve", reads, writes, marks)
        inst = getattr(self.nc.vector, fn)(*args, **kw)
        m = self._inc(inst, "dve")
        self._post(m, reads, writes)
        return m

    def a(self, fn, *args, marks=(), reads=(), writes=(), **kw):
        self._wait1("act", ("act", self.cnt["act"]))
        self._pre("act", reads, writes, marks)
        inst = getattr(self.nc.scalar, fn)(*args, **kw)
        m = self._inc(inst, "act")
        self._post(m, reads, writes)
        return m

    def pe_mm(self, bank, lhsT, rhs, start, stop, rbufs=(), marks=(),
              inc=False, out=None):
        if start:
            self.waits("pe", bank.readers)
            bank.readers = []
        for b in rbufs:
            self._wait1("pe", b.writer)
        bank.pending.extend(rbufs)
        self.waits("pe", marks)
        inst = self.nc.tensor.matmul(out if out is not None else bank.t,
                                     lhsT, rhs, start=start, stop=stop)
        if stop or inc:
            m = self._inc(inst, "pe")
            bank.writer = m
            for b in bank.pending:
                b.readers.append(m)
            bank.pending = []
            return m
        return None

    def pe_tr(self, bank, out_ap, in_buf, in_ap, ident_buf, ident,
              marks=()):
        self.waits("pe", bank.readers)
        bank.readers = []
        for b in (in_buf, ident_buf):
            if b is not None:
                self._wait1("pe", b.writer)
        self.waits("pe", marks)
        inst = self.nc.tensor.transpose(out_ap, in_ap, ident)
        m = self._inc(inst, "pe")
        bank.writer = m
        for b in (in_buf, ident_buf):
            if b is not None:
                b.readers.append(m)
        return m

    def dma(self, q, dst, src, reads=(), writes=(), marks=()):
        eng = self.eng[q]
        for b in reads:
            mk = b.writer
            self._dwait(q, eng, mk)
        for w in writes:
            self._dwait(q, eng, w.writer)
            for mk in w.readers:
                self._dwait(q, eng, mk)
        for mk in marks:
            self._dwait(q, eng, mk)
        pool = self.DPOOL[q]
        r = self.drot_q.setdefault(q, 0)
        idx = pool[r]
        self.drot_q[q] = (r + 1) % len(pool)
        if self.dcnt[idx]:
            self._dwait(q, eng, ("d", idx, self.dcnt[idx]))
        self.dcnt[idx] += 16
        eng.dma_start(dst, src).then_inc(self.dsem[idx], 16)
        m = ("d", idx, self.dcnt[idx])
        for b in reads:
            b.readers.append(m)
        for w in writes:
            w.writer = m
            w.readers = []
        return m

    def _dwait(self, q, eng, mk):
        if mk is None:
            return
        if mk[0] == "d":
            _, idx, val = mk
            key = (q, "d", idx)
            sem = self.dsem[idx]
        else:
            name, val = mk
            key = (q, name)
            sem = self.sem[name]
        if self.waited.get(key, 0) < val:
            eng.wait_ge(sem, val)
            self.waited[key] = val

    def final_dma_wait(self):
        for idx in range(self.NDMA):
            if self.dcnt[idx]:
                self.nc.sync.wait_ge(self.dsem[idx], self.dcnt[idx])

    def alias(self, dst_bufs, src_bufs):
        marks = []
        for s in src_bufs:
            if s.writer is not None:
                marks.append(s.writer)
            marks.extend(s.readers)
        for d in dst_bufs:
            d.writer = None
            d.readers = list(marks)


class BankPool:
    def __init__(self, banks):
        self.banks = banks
        self.i = 0

    def take(self):
        b = self.banks[self.i]
        self.i = (self.i + 1) % len(self.banks)
        return b


class Ring:

    def __init__(self, pk, views):
        self.pk = pk
        self.bufs = [Buf(v, f"slot{i}") for i, v in enumerate(views)]
        self.i = 0

    def load(self, src, kc, cols):
        b = self.bufs[self.i]
        self.i = (self.i + 1) % len(self.bufs)
        view = b.t[:, 0:kc * cols].rearrange("p (k c) -> p k c", c=cols)
        self.pk.dma("gp", view, src, writes=[b])
        return {"view": view, "buf": b}


def _build(upto="full", dump=None, ffn_lim=99):
    nc = bass.Bass("TRN2", target_bir_lowering=False)
    stack = contextlib.ExitStack()

    XIN = nc.dram_tensor("XIN", [128, KC, S], F32, kind="ExternalInput").ap()
    CC = nc.dram_tensor("CC", [128, 16], F32, kind="ExternalInput").ap()
    CONSTS = nc.dram_tensor("CONSTS", [128, 256], F16, kind="ExternalInput").ap()
    SEL = nc.dram_tensor("SEL", [8, 1024], F16, kind="ExternalInput").ap()
    WQ = nc.dram_tensor("WQ", [16, 128, 16, 128], F32, kind="ExternalInput").ap()
    WK = nc.dram_tensor("WK", [16, 128, 16, 128], F32, kind="ExternalInput").ap()
    WV = nc.dram_tensor("WV", [16, 128, 16, 128], F32, kind="ExternalInput").ap()
    WO = nc.dram_tensor("WO", [16, 128, 16, 128], F32, kind="ExternalInput").ap()
    WMOD = nc.dram_tensor("WMOD", [12, 16, 128, 1024], F32, kind="ExternalInput").ap()
    GUP = nc.dram_tensor("GUP", [8, 22, 128, 16, 128], F32, kind="ExternalInput").ap()
    DWN = nc.dram_tensor("DWN", [8, 16, 128, 11, 128], F32, kind="ExternalInput").ap()
    SHGU = nc.dram_tensor("SHGU", [88, 128, 16, 128], F32, kind="ExternalInput").ap()
    SHD = nc.dram_tensor("SHD", [16, 4, 128, 11, 128], F32, kind="ExternalInput").ap()
    RT = nc.dram_tensor("RT", [128, 128], F32, kind="ExternalInput").ap()
    SGW = nc.dram_tensor("SGW", [128, 16], F32, kind="ExternalInput").ap()
    OUT = nc.dram_tensor("OUT", [128, KC, S], F32, kind="ExternalOutput").ap()
    XSP = nc.dram_tensor("XSP", [128, KC, S], F16, kind="Internal").ap()
    DBG = None
    if dump is not None:
        DBG = nc.dram_tensor("DBG", [128, KC, S], F32, kind="ExternalOutput").ap()

    xar = nc.alloc_sbuf_tensor("xar", [128, 65536], I8).ap()
    r1 = nc.alloc_sbuf_tensor("r1", [128, 65536], I8).ap()
    r2 = nc.alloc_sbuf_tensor("r2", [128, 32768], I8).ap()
    wring_t = nc.alloc_sbuf_tensor("wring", [128, 4 * 4096], I8).ap()

    def carve(region, off, dtype, *shape):
        n = int(np.prod(shape)) * (2 if dtype == F16 else 4)
        ap = region[:, off:off + n].bitcast(dtype)
        if len(shape) > 1:
            ap = ap.rearrange("p (a b) -> p a b", b=shape[-1])
        return ap

    x3 = carve(xar, 0, F16, KC, S)
    h1t = carve(r1, 0, F16, KC, 1024)
    qt = carve(r1, 32768, F16, KC, 1024)
    kv3 = carve(r2, 0, F16, KC, 1024)
    acc3 = carve(xar, 0, F16, KC, 1024)
    hidt = [carve(xar, 32768, F16, IC, 1024), carve(r2, 0, F16, IC, 1024)]
    sgt = carve(xar, 55296, F16, 1024)
    cbct = carve(xar, 57344, F16, 1024)
    sbct = carve(xar, 59392, F16, 1024)
    h23 = carve(r1, 0, F16, KC, S)

    def sb(name, shape, dtype):
        return Buf(nc.alloc_sbuf_tensor(name, shape, dtype).ap(), name)

    xB = [Buf(x3, "x3h0"), Buf(x3, "x3h1")]
    h1B = Buf(h1t, "h1t")
    qB = Buf(qt, "qt")
    kvB = Buf(kv3, "kv3")
    accB = Buf(acc3, "acc3")
    hidB = [Buf(hidt[0], "hid0"), Buf(hidt[1], "hid1")]
    sgB = [Buf(sgt[:, 0:512], "sgt0"), Buf(sgt[:, 512:1024], "sgt1")]
    cbB = Buf(cbct, "cbc")
    sbB = Buf(sbct, "sbc")
    h2B = [Buf(h23, "h23a"), Buf(h23, "h23b")]

    consB = sb("cons", [128, 256], F16)
    ident = consB.t[:, 0:128]
    ones1 = consB.t[:, 128:256]
    crawB = sb("c_raw", [128, 16], F32)
    silucB = sb("silu_c", [128, 16], F16)
    modvB = sb("modv", [128, 96], F32)
    modv = modvB.t
    carryB = sb("carry", [128, 16], F32)
    lnAB = sb("lnA", [128, 512], F16)
    lnBB = sb("lnB", [128, 512], F16)
    lnTB = [sb("lnT0", [128, 512], F16), sb("lnT1", [128, 512], F16)]
    lnrow_t = nc.alloc_sbuf_tensor("lnrow", [1, 2, 512], F16).ap()
    lnrB = [Buf(lnrow_t[:, 0, :], "lnr0"), Buf(lnrow_t[:, 1, :], "lnr1")]
    lnFB = sb("lnF", [128, 512], F32)
    stg_t = nc.alloc_sbuf_tensor("stg", [128, 2, 512], F16).ap()
    stg2 = stg_t[:, 0:2, :]
    stgB = [Buf(stg_t[:, 0, :], "stg0"), Buf(stg_t[:, 1, :], "stg1")]
    mrowB = sb("mrow", [1, 512], F16)
    lgB = sb("lg", [8, 512], F16)
    rtB = sb("rt8", [128, 8], F32)
    rpB = sb("rp8", [128, 8], F32)
    rmB = sb("rm8", [128, 8], F32)
    rkB = sb("rk8", [128, 8], F32)
    cmbB = sb("cmb16", [128, 8], F16)
    m1B = sb("m1c", [128, 1], F32)
    m2B = sb("m2c", [128, 1], F32)
    negmB = sb("negm", [128, 1], F32)
    denB = sb("denc", [128, 1], F32)
    crowB = sb("crow", [8, 1024], F16)
    srowB = sb("srow", [1, 1024], F16)
    selB = sb("sel", [8, 1024], F16)
    rtwB = sb("rt_w", [128, 128], F16)
    sgwB = sb("sg_w", [128, 16], F16)
    x1rB = [sb("x1r0", [128, 1024], F16), sb("x1r1", [128, 1024], F16)]
    orB = [sb(f"oring{i}", [128, 512], F32) for i in range(2)]
    tfinB = sb("tfin", [128, 512], F32)

    psb = [Buf(stack.enter_context(
        nc.psum_tensor(f"ps{i}", [128, 512], F32)).ap(), f"ps{i}")
        for i in range(7)]
    pstr_t = stack.enter_context(nc.psum_tensor("pstr", [128, 1024], F16)).ap()
    pstrA = Buf(pstr_t[:, 0:512], "pstrA")
    pstrB = pstrA

    pk = PK(nc, stack)
    pool = BankPool(psb)
    ring = Ring(pk, [wring_t[:, s * 4096:(s + 1) * 4096].bitcast(F16)
                     for s in range(4)])

    pk.dma("sy", consB.t, CONSTS, writes=[consB])
    pk.dma("sy", crawB.t, CC, writes=[crawB])
    pk.dma("sy", selB.t, SEL, writes=[selB])
    pk.dma("gp", rtwB.t, RT, writes=[rtwB])
    pk.dma("gp", sgwB.t, SGW, writes=[sgwB])
    pk.a("activation", silucB.t, crawB.t, AF.Silu,
         reads=[crawB], writes=[silucB])
    pk.v("memset", carryB.t, 0.0, writes=[carryB])

    def emit_mod(fs_range):
        for fs in fs_range:
            v = fs // 2
            cbase = (fs % 2) * 8
            bA, bB = pool.take(), pool.take()
            for k in range(KC):
                p = ring.load(WMOD[fs, k], 1, 1024)
                for h, bank in ((0, bA), (1, bB)):
                    pk.pe_mm(bank, silucB.t[:, k:k + 1],
                             p["view"][:, 0, h * 512:(h + 1) * 512],
                             start=(k == 0), stop=(k == KC - 1),
                             rbufs=[p["buf"], silucB],
                             out=bank.t[0:1, :], inc=(h == 1))
            for h, bank in ((0, bA), (1, bB)):
                pk.a("copy", mrowB.t, bank.t[0:1, :], reads=[bank],
                     writes=[mrowB])
                for j in range(4):
                    pk.pe_tr(pstrA, pstrA.t[:, 0:1], mrowB,
                             mrowB.t[0:1, j * 128:(j + 1) * 128],
                             consB, ident[0:1, 0:1])
                    col = v * 16 + cbase + h * 4 + j
                    pk.a("copy", modv[:, col:col + 1], pstrA.t[:, 0:1],
                         reads=[pstrA], writes=[modvB])

    with nc.named_scope("modA"):
        emit_mod(range(2))
        pk.dma("gp", x3, XIN, writes=xB)
        emit_mod(range(2, 4))
    if upto == "modraw":
        emit_mod(range(4, 12))

    def emit_gates(vs):
        for vname in vs:
            c0 = MODC[vname]
            sl = modv[:, c0:c0 + 16]
            if vname in ("ca", "cb"):
                pk.v("tensor_single_scalar", sl, sl, 1.0, ALU.add,
                     reads=[modvB], writes=[modvB])
            else:
                sgn = lnFB.t[:, 0:16]
                mag = lnFB.t[:, 16:32]
                pk.v("tensor_single_scalar", sgn, sl, 0.0, ALU.is_ge,
                     reads=[modvB], writes=[lnFB])
                pk.v("tensor_scalar", sgn, sgn, 2.0, -1.0,
                     op0=ALU.mult, op1=ALU.add,
                     reads=[lnFB], writes=[lnFB])
                pk.a("activation", mag, sl, AF.Abs,
                     reads=[modvB], writes=[lnFB])
                pk.v("tensor_single_scalar", mag, mag, 0.25, ALU.max,
                     reads=[lnFB], writes=[lnFB])
                pk.v("tensor_mul", sl, sgn, mag,
                     reads=[lnFB], writes=[modvB])

    if upto != "modraw":
        emit_gates(["ca"])

    def emit_ln(srcB, src3, dstB, dst3, ca_col, sa_col, s_lo, s_n, d_lo=None):
        if d_lo is None:
            d_lo = s_lo
        subs = list(range(s_n // 512))
        state = {}

        def stats(sub):
            s0 = s_lo + sub * 512
            sB_half = srcB[s0 // 1024] if isinstance(srcB, list) else srcB
            bA, bB = pool.take(), pool.take()
            for k in range(0, KC, 2):
                xs2 = src3[:, k:k + 2, s0:s0 + 512]
                pk.v("tensor_mul", stg2, xs2, xs2, reads=[sB_half],
                     writes=[stgB[0], stgB[1]])
                for kk in (k, k + 1):
                    xs = src3[:, kk, s0:s0 + 512]
                    pk.pe_mm(bA, ones1, xs, start=(kk == 0),
                             stop=(kk == KC - 1), rbufs=[consB, sB_half])
                    pk.pe_mm(bB, ones1, stgB[kk % 2].t,
                             start=(kk == 0), stop=(kk == KC - 1),
                             rbufs=[consB, stgB[kk % 2]], inc=True)
            state[sub] = (bA, bB, s0, s_lo and 0)

        def tail(sub):
            bA, bB, s0, _ = state.pop(sub)
            d0 = d_lo + sub * 512
            sB_half = srcB[s0 // 1024] if isinstance(srcB, list) else srcB
            dB_half = dstB[(d_lo + sub * 512) // 1024] \
                if isinstance(dstB, list) else dstB
            r0, r1 = lnrB
            r2 = mrowB
            r3s = srowB.t[0:1, 0:512]
            pk.a("activation", r0.t, bA.t[0:1, :], AF.Identity,
                 scale=1.0 / 2048.0, reads=[bA], writes=[r0])
            pk.a("activation", r1.t, bB.t[0:1, :], AF.Identity,
                 scale=1.0 / 2048.0, reads=[bB], writes=[r1])
            pk.v("tensor_mul", r2.t, r0.t, r0.t, reads=[r0], writes=[mrowB])
            pk.v("tensor_tensor", r1.t, r1.t, r2.t, op=ALU.subtract,
                 reads=[r1, mrowB], writes=[r1])
            pk.v("tensor_single_scalar", r1.t, r1.t, EPS, ALU.add,
                 reads=[r1], writes=[r1])
            with nc.allow_low_precision(reason="f16 LN row scratch, 5e-4 rel"):
                pk.v("reciprocal", r2.t, r1.t, reads=[r1], writes=[mrowB])
            pk.a("sqrt", r3s, r2.t, reads=[mrowB], writes=[srowB])
            pk.v("scalar_tensor_tensor", r1.t, r0.t, -1.0, r3s,
                 op0=ALU.mult, op1=ALU.mult,
                 reads=[r0, srowB], writes=[r1])
            bc1, bc2 = pool.take(), pool.take()
            pk.pe_mm(bc1, ones1[0:1, :], r3s, start=True, stop=True,
                     rbufs=[consB, srowB])
            pk.pe_mm(bc2, ones1[0:1, :], r1.t, start=True, stop=True,
                     rbufs=[consB, r1])
            pk.v("tensor_copy", lnAB.t, bc1.t, reads=[bc1], writes=[lnAB])
            pk.v("tensor_copy", lnBB.t, bc2.t, reads=[bc2], writes=[lnBB])
            for k in range(KC):
                tB = lnTB[k % 2]
                pk.v("tensor_mul", tB.t, src3[:, k, s0:s0 + 512], lnAB.t,
                     reads=[sB_half, lnAB], writes=[tB])
                pk.v("tensor_tensor", tB.t, tB.t, lnBB.t, op=ALU.add,
                     reads=[tB, lnBB], writes=[tB])
                pk.a("activation", dst3[:, k, d0:d0 + 512], tB.t,
                     AF.Identity,
                     bias=modv[:, sa_col + k:sa_col + k + 1],
                     scale=modv[:, ca_col + k:ca_col + k + 1],
                     reads=[tB, modvB], writes=[dB_half])

        stats(subs[0])
        for sub in subs[1:]:
            stats(sub)
            tail(sub - 1)
        tail(subs[-1])

    def emit_qkv(t, fuse_scan=True):
        for f in range(16):
            pq = ring.load(WQ[f], KC, 128)
            pkk = ring.load(WK[f], KC, 128)
            pv = ring.load(WV[f], KC, 128)
            bq = [pool.take(), pool.take()]
            bk = [pool.take(), pool.take()]
            bv = [pool.take(), pool.take()]
            for p, bb in ((pq, bq), (pkk, bk), (pv, bv)):
                for st in range(2):
                    for k in range(KC):
                        pk.pe_mm(bb[st], p["view"][:, k, :],
                                 h1t[:, k, st * 512:(st + 1) * 512],
                                 start=(k == 0), stop=(k == KC - 1),
                                 rbufs=[p["buf"], h1B])
            for st in range(2):
                pk.a("copy", qt[:, f, st * 512:(st + 1) * 512], bq[st].t,
                     reads=[bq[st]], writes=[qB])
                pk.a("copy", stgB[st].t, bk[st].t, reads=[bk[st]],
                     writes=[stgB[st]])
                pk.v("tensor_mul", kv3[:, f, st * 512:(st + 1) * 512],
                     stgB[st].t, bv[st].t,
                     reads=[stgB[st], bv[st]], writes=[kvB])
            if fuse_scan:
                k = f
                pk.v("tensor_tensor_scan", kv3[:, k, :], kv3[:, k, :],
                     ones1[:, 0:1].broadcast_to([128, 1024]),
                     carryB.t[:, k:k + 1], op0=ALU.add, op1=ALU.mult,
                     reads=[kvB, consB, carryB], writes=[kvB])
                pk.v("tensor_copy", carryB.t[:, k:k + 1],
                     kv3[:, k, 1023:1024], reads=[kvB], writes=[carryB])
                pk.v("scalar_tensor_tensor", qt[:, k, :], qt[:, k, :],
                     DENOM, kv3[:, k, :], op0=ALU.mult, op1=ALU.mult,
                     reads=[qB, kvB], writes=[qB])

    def emit_scan_vhat():
        for k in range(KC):
            pk.v("tensor_tensor_scan", kv3[:, k, :], kv3[:, k, :],
                 ones1[:, 0:1].broadcast_to([128, 1024]),
                 carryB.t[:, k:k + 1], op0=ALU.add, op1=ALU.mult,
                 reads=[kvB, consB, carryB], writes=[kvB])
            pk.v("tensor_copy", carryB.t[:, k:k + 1], kv3[:, k, 1023:1024],
                 reads=[kvB], writes=[carryB])
            pk.v("scalar_tensor_tensor", qt[:, k, :], qt[:, k, :],
                 DENOM, kv3[:, k, :], op0=ALU.mult, op1=ALU.mult,
                 reads=[qB, kvB], writes=[qB])

    def emit_wo(t):
        s0 = t * 1024
        for f in range(16):
            po = ring.load(WO[f], KC, 128)
            for st in range(2):
                b = pool.take()
                for k in range(KC):
                    pk.pe_mm(b, po["view"][:, k, :],
                             qt[:, k, st * 512:(st + 1) * 512],
                             start=(k == 0), stop=(k == KC - 1),
                             rbufs=[po["buf"], qB])
                ga_c = MODC["ga"] + f
                xs = x3[:, f, s0 + st * 512:s0 + (st + 1) * 512]
                pk.v("scalar_tensor_tensor", xs, b.t,
                     modv[:, ga_c:ga_c + 1], xs,
                     op0=ALU.mult, op1=ALU.add,
                     reads=[b, modvB], writes=[xB[t]])

    m_spill = None
    if upto in ("attn", "ln2", "full"):
        with nc.named_scope("ln1a"):
            emit_ln(xB, x3, h1B, h1t, MODC["ca"], MODC["sa"], 0, 1024)
        with nc.named_scope("qkv0"):
            emit_qkv(0)
        with nc.named_scope("modA2"):
            emit_mod(range(4, 6))
            emit_gates(["ga"])
        with nc.named_scope("ln1b"):
            emit_ln(xB, x3, h1B, h1t, MODC["ca"], MODC["sa"], 1024, 1024,
                    d_lo=0)
        with nc.named_scope("wo0"):
            emit_wo(0)
        with nc.named_scope("qkv1"):
            emit_qkv(1)
        with nc.named_scope("modB"):
            emit_mod(range(6, 12))
            emit_gates(["cb", "gb"])
        with nc.named_scope("wo1"):
            emit_wo(1)

    if upto in ("ln2", "full"):
        pk.alias(h2B, [h1B, qB])
        with nc.named_scope("ln2"):
            emit_ln(xB, x3, h2B, h23, MODC["cb"], MODC["sb"], 0, 2048)
        m_spill = pk.dma("sy", XSP, x3, reads=xB)
        pk.alias([accB, hidB[0], sgB[0], sgB[1], cbB, sbB], xB)
        pk.alias([hidB[1]], [kvB])

    def gu_src(pe_i, j):
        if pe_i < 4:
            return SHGU[22 * pe_i + j]
        return GUP[pe_i - 4, j]

    def dn_src(pe_i, f):
        if pe_i < 4:
            return SHD[f, pe_i]
        return DWN[pe_i - 4, f]

    def emit_ffn_half(hf):
        s0 = hf * 1024
        for st in range(2 if ffn_lim >= -3 else 0):
            b = pool.take()
            for k in range(KC):
                pk.pe_mm(b, rtwB.t[:, k * 8:(k + 1) * 8],
                         h23[:, k, s0 + st * 512:s0 + (st + 1) * 512],
                         start=(k == 0), stop=(k == KC - 1),
                         rbufs=[rtwB, h2B[hf]], out=b.t[0:8, :])
            pk.a("copy", lgB.t, b.t[0:8, :], reads=[b], writes=[lgB])
            for j in range(4 if ffn_lim >= -2 else 0):
                pk.pe_tr(pstrA, pstrA.t[0:128, 0:8], lgB,
                         lgB.t[0:8, j * 128:(j + 1) * 128],
                         consB, ident[0:8, 0:8])
                pk.v("tensor_copy", rtB.t, pstrA.t[0:128, 0:8],
                     reads=[pstrA], writes=[rtB])
                pk.v("reduce_max", m1B.t, rtB.t, axis=AX,
                     reads=[rtB], writes=[m1B])
                pk.v("tensor_single_scalar", negmB.t, m1B.t, -1.0,
                     ALU.mult, reads=[m1B], writes=[negmB])
                pk.v("tensor_single_scalar", rkB.t, rtB.t, m1B.t,
                     ALU.is_ge, reads=[rtB, m1B], writes=[rkB])
                pk.v("scalar_tensor_tensor", rpB.t, rkB.t, -1e30, rtB.t,
                     op0=ALU.mult, op1=ALU.add,
                     reads=[rkB, rtB], writes=[rpB])
                pk.v("reduce_max", m2B.t, rpB.t, axis=AX,
                     reads=[rpB], writes=[m2B])
                pk.a("activation", rpB.t, rtB.t, AF.Exp, bias=negmB.t,
                     reads=[rtB, negmB], writes=[rpB])
                pk.v("tensor_single_scalar", rkB.t, rtB.t, m2B.t,
                     ALU.is_ge, reads=[rtB, m2B], writes=[rkB])
                pk.v("tensor_mul", rmB.t, rpB.t, rkB.t,
                     reads=[rpB, rkB], writes=[rmB])
                pk.v("reduce_sum", denB.t, rmB.t, axis=AX,
                     reads=[rmB], writes=[denB])
                pk.v("reciprocal", denB.t, denB.t,
                     reads=[denB], writes=[denB])
                pk.v("tensor_single_scalar", cmbB.t, rmB.t, denB.t,
                     ALU.mult, reads=[rmB, denB], writes=[cmbB])
                if ffn_lim >= -1:
                    pk.pe_tr(pstrB, pstr_t[0:8, 512:640], cmbB, cmbB.t,
                             consB, ident)
                    pk.a("copy",
                         crowB.t[0:8, st * 512 + j * 128:st * 512 + (j + 1) * 128],
                         pstr_t[0:8, 512:640], reads=[pstrB], writes=[crowB])
        for st in range(2 if ffn_lim >= 0 else 0):
            b = pool.take()
            for k in range(KC):
                pk.pe_mm(b, sgwB.t[:, k:k + 1],
                         h23[:, k, s0 + st * 512:s0 + (st + 1) * 512],
                         start=(k == 0), stop=(k == KC - 1),
                         rbufs=[sgwB, h2B[hf]], out=b.t[0:1, :])
            pk.a("activation", srowB.t[0:1, st * 512:(st + 1) * 512],
                 b.t[0:1, :], AF.Sigmoid, reads=[b], writes=[srowB])
            b2 = pool.take()
            pk.pe_mm(b2, ones1[0:1, :],
                     srowB.t[0:1, st * 512:(st + 1) * 512],
                     start=True, stop=True, rbufs=[consB, srowB])
            pk.v("tensor_copy", sbct[:, st * 512:(st + 1) * 512], b2.t,
                 reads=[b2], writes=[sbB])
        for pe_i in range(min(NPE, ffn_lim)):
            scope = stack.enter_context(nc.named_scope(f"pe{hf}_{pe_i}")) if False else None
            slot = pe_i % 2
            if pe_i < 4:
                scB, sct = sbB, sbct
            else:
                e = pe_i - 4
                for st in range(2):
                    b2 = pool.take()
                    pk.pe_mm(b2, selB.t[:, e * 128:(e + 1) * 128],
                             crowB.t[0:8, st * 512:(st + 1) * 512],
                             start=True, stop=True, rbufs=[selB, crowB])
                    pk.v("tensor_copy", cbct[:, st * 512:(st + 1) * 512],
                         b2.t, reads=[b2], writes=[cbB])
                scB, sct = cbB, cbct
            gu_scope = nc.named_scope(f"gu{hf}_{pe_i}")
            gu_scope.__enter__()
            for i in range(IC):
                pg = ring.load(gu_src(pe_i, 2 * i), KC, 128)
                pu = ring.load(gu_src(pe_i, 2 * i + 1), KC, 128)
                bg = [pool.take(), pool.take()]
                bu = [pool.take(), pool.take()]
                for p, bb in ((pg, bg), (pu, bu)):
                    for st in range(2):
                        for k in range(KC):
                            pk.pe_mm(bb[st], p["view"][:, k, :],
                                     h23[:, k, s0 + st * 512:s0 + (st + 1) * 512],
                                     start=(k == 0), stop=(k == KC - 1),
                                     rbufs=[p["buf"], h2B[hf]])
                for st in range(2):
                    pk.a("activation", sgt[:, st * 512:(st + 1) * 512],
                         bg[st].t, AF.Silu, reads=[bg[st]], writes=[sgB[st]])
                    hv = hidt[slot][:, i, st * 512:(st + 1) * 512]
                    pk.v("tensor_mul", hv, bu[st].t,
                         sct[:, st * 512:(st + 1) * 512],
                         reads=[bu[st], scB], writes=[hidB[slot]])
                    pk.v("tensor_mul", hv, hv,
                         sgt[:, st * 512:(st + 1) * 512],
                         reads=[hidB[slot], sgB[st]], writes=[hidB[slot]])
            gu_scope.__exit__(None, None, None)
            if pe_i % 2 == 1:
                r = pe_i // 2
                peA, peB = pe_i - 1, pe_i
                dn_scope = nc.named_scope(f"dn{hf}_{r}")
                dn_scope.__enter__()
                for f in range(16):
                    pa = ring.load(dn_src(peA, f), IC, 128)
                    pb = ring.load(dn_src(peB, f), IC, 128)
                    if r == 5:
                        pk.dma("sy", x1rB[f % 2].t,
                               XSP[:, f, s0:s0 + 1024],
                               marks=[m_spill], writes=[x1rB[f % 2]])
                    for st in range(2):
                        b = pool.take()
                        for k in range(IC):
                            pk.pe_mm(b, pa["view"][:, k, :],
                                     hidt[peA % 2][:, k, st * 512:(st + 1) * 512],
                                     start=(k == 0), stop=False,
                                     rbufs=[pa["buf"], hidB[peA % 2]])
                        for k in range(IC):
                            pk.pe_mm(b, pb["view"][:, k, :],
                                     hidt[peB % 2][:, k, st * 512:(st + 1) * 512],
                                     start=False, stop=(k == IC - 1),
                                     rbufs=[pb["buf"], hidB[peB % 2]])
                        a_sl = acc3[:, f, st * 512:(st + 1) * 512]
                        if r == 0:
                            pk.a("copy", a_sl, b.t, reads=[b], writes=[accB])
                        elif r < 5:
                            pk.v("tensor_tensor", a_sl, b.t, a_sl,
                                 op=ALU.add, reads=[b, accB], writes=[accB])
                        else:
                            gb_c = MODC["gb"] + f
                            pk.v("tensor_tensor", tfinB.t, b.t, a_sl,
                                 op=ALU.add, reads=[b, accB], writes=[tfinB])
                            sidx = (f * 2 + st) % 2
                            pk.v("scalar_tensor_tensor", orB[sidx].t,
                                 tfinB.t, modv[:, gb_c:gb_c + 1],
                                 x1rB[f % 2].t[:, st * 512:(st + 1) * 512],
                                 op0=ALU.mult, op1=ALU.add,
                                 reads=[tfinB, modvB, x1rB[f % 2]],
                                 writes=[orB[sidx]])
                            pk.dma("sy",
                                   OUT[:, f, s0 + st * 512:s0 + (st + 1) * 512],
                                   orB[sidx].t, reads=[orB[sidx]])
                _dn_last = (f == 15)
                if _dn_last:
                    dn_scope.__exit__(None, None, None)

    if upto == "full":
        for hf in range(2 if ffn_lim >= 99 else 1):
            with nc.named_scope(f"ffn{hf}"):
                emit_ffn_half(hf)

        if ffn_lim >= 99:
            pk.final_dma_wait()
        else:
            zm = pk.v("memset", lnTB[0].t, 0.0,
                      marks=[("dve", pk.cnt["dve"]), ("act", pk.cnt["act"]),
                             ("pe", pk.cnt["pe"])], writes=[lnTB[0]])
            for kk in range(KC):
                for st4 in range(4):
                    pk.dma("sy", OUT[:, kk, st4 * 512:(st4 + 1) * 512],
                           lnTB[0].t, reads=[lnTB[0]])
            pk.final_dma_wait()

    if dump is not None:
        allm = [("dve", pk.cnt["dve"]), ("act", pk.cnt["act"]),
                ("pe", pk.cnt["pe"])]
        if dump == "mod":
            pk.dma("sy", DBG[:, 0, 0:96], modv, marks=allm)
        elif dump == "x1":
            pk.dma("gp", DBG, x3, marks=allm)
        elif dump == "h1":
            pk.dma("gp", DBG[:, :, 0:1024], h1t, marks=allm)
        elif dump == "kv":
            pk.dma("gp", DBG[:, :, 0:1024], kv3, marks=allm)
        elif dump == "h2":
            pk.dma("gp", DBG, h23, marks=allm)
        pk.final_dma_wait()
        if upto != "full":
            zm = pk.v("memset", lnTB[0].t, 0.0, marks=allm,
                      writes=[lnTB[0]])
            for kk in range(KC):
                for st4 in range(4):
                    pk.dma("sy", OUT[:, kk, st4 * 512:(st4 + 1) * 512],
                           lnTB[0].t, reads=[lnTB[0]])
            pk.final_dma_wait()

    stack.close()
    return nc


_NC_CACHE = {}


def _get_nc(upto="full", dump=None, ffn_lim=99):
    key = (upto, dump, ffn_lim)
    if key not in _NC_CACHE:
        _NC_CACHE[key] = _build(upto, dump, ffn_lim)
    return _NC_CACHE[key]



def _pan2048(W):
    A = np.ascontiguousarray(np.asarray(W, dtype=np.float32).T)
    return np.ascontiguousarray(
        A.reshape(16, 128, 16, 128).transpose(2, 1, 0, 3))


def _prep_inputs(ins):
    x = np.asarray(ins["L0_attn_x"], dtype=np.float32)
    c = np.asarray(ins["L0_c"], dtype=np.float32)
    wmod = np.asarray(ins["L0_wmod"], dtype=np.float32)
    gup = np.asarray(ins["L0_ffn_gate_up_proj"], dtype=np.float32)
    dwn = np.asarray(ins["L0_ffn_down_proj"], dtype=np.float32)
    shg = np.asarray(ins["L0_ffn_sh_gate_w"], dtype=np.float32)
    shu = np.asarray(ins["L0_ffn_sh_up_w"], dtype=np.float32)
    shd = np.asarray(ins["L0_ffn_sh_down_w"], dtype=np.float32)
    rw = np.asarray(ins["L0_ffn_router_weight"], dtype=np.float32)
    sgw = np.asarray(ins["L0_ffn_shared_gate_w"], dtype=np.float32)

    shared = {
        "WQ": _pan2048(ins["L0_attn_Wq"]),
        "WK": _pan2048(ins["L0_attn_Wk"]),
        "WV": _pan2048(ins["L0_attn_Wv"]),
        "WO": _pan2048(ins["L0_attn_Wo"]),
        "WMOD": np.ascontiguousarray(
            wmod.reshape(16, 128, 12, 1024).transpose(2, 0, 1, 3)),
        "RT": np.ascontiguousarray(
            rw.T.reshape(16, 128, 8).transpose(1, 0, 2).reshape(128, 128)),
        "SGW": np.ascontiguousarray(sgw[0].reshape(16, 128).T),
        "CONSTS": np.concatenate(
            [np.eye(128, dtype=np.float32),
             np.ones((128, 128), np.float32)], axis=1).astype(np.float16),
        "SEL": np.repeat(np.eye(8, dtype=np.float32), 128,
                         axis=1).astype(np.float16),
    }
    idx = []
    for i in range(11):
        idx += [i, 11 + i]
    g8 = []
    for e in range(8):
        A = np.ascontiguousarray(gup[e].T)
        g8.append(A.reshape(16, 128, 22, 128).transpose(2, 1, 0, 3)[idx])
    shared["GUP"] = np.ascontiguousarray(np.stack(g8))
    d8 = []
    for e in range(8):
        A = np.ascontiguousarray(dwn[e].T)
        d8.append(A.reshape(11, 128, 16, 128).transpose(2, 1, 0, 3))
    shared["DWN"] = np.ascontiguousarray(np.stack(d8))
    Hg = np.ascontiguousarray(shg.T).reshape(
        16, 128, 44, 128).transpose(2, 1, 0, 3)
    Hu = np.ascontiguousarray(shu.T).reshape(
        16, 128, 44, 128).transpose(2, 1, 0, 3)
    Hgu = np.empty((88,) + Hg.shape[1:], dtype=np.float32)
    Hgu[0::2] = Hg
    Hgu[1::2] = Hu
    shared["SHGU"] = np.ascontiguousarray(Hgu)
    Ad = np.ascontiguousarray(shd.T)
    shared["SHD"] = np.ascontiguousarray(
        Ad.reshape(4, 11, 128, 16, 128).transpose(3, 0, 2, 1, 4))

    in_maps = []
    for b in range(8):
        m = dict(shared)
        m["XIN"] = np.ascontiguousarray(
            x[b].T.reshape(16, 128, 2048).transpose(1, 0, 2))
        m["CC"] = np.ascontiguousarray(c[b].reshape(16, 128).T)
        in_maps.append(m)
    return in_maps


def kernel(**ins):
    in_maps = _prep_inputs(ins)
    nc = _get_nc()
    res = run_bass_kernel_spmd(nc, in_maps, core_ids=list(range(8)))
    outs = []
    for b in range(8):
        o = np.asarray(res.results[b]["OUT"])
        outs.append(o.transpose(1, 0, 2).reshape(2048, 2048).T)
    return np.ascontiguousarray(np.stack(outs)).astype(np.float32)


# version 09
# speedup vs baseline: 1.1448x; 1.1448x over previous
import contextlib
import numpy as np

import concourse.bass as bass
import concourse.mybir as mybir
from concourse.bass_utils import run_bass_kernel_spmd

F32 = mybir.dt.float32
F16 = mybir.dt.float16
I8 = mybir.dt.int8
AF = mybir.ActivationFunctionType
ALU = mybir.AluOpType
AX = mybir.AxisListType.X

KC = 16
S = 2048
EPS = 1e-5
DENOM = float(np.sqrt(np.float32(2048.0)) / np.float32(2048.0))
IC = 11
NPE = 12

MODC = {"sa": 0, "ca": 16, "ga": 32, "sb": 48, "cb": 64, "gb": 80}


class Buf:

    def __init__(self, t=None, name=""):
        self.t = t
        self.name = name
        self.writer = None
        self.readers = []
        self.pending = []


class PK:
    NDMA = 24
    DPOOL = {"gp": list(range(16)), "sy": list(range(16, 24))}

    def __init__(self, nc, stack):
        self.nc = nc
        self.sem = {}
        for name in ("pe", "dve", "act"):
            self.sem[name] = stack.enter_context(nc.semaphore(name))
        self.dsem = [stack.enter_context(nc.semaphore(f"dm{i}"))
                     for i in range(self.NDMA)]
        self.cnt = {"pe": 0, "dve": 0, "act": 0}
        self.dcnt = [0] * self.NDMA
        self.drot_q = {}
        self.waited = {}
        self.eng = {"pe": nc.tensor, "dve": nc.vector, "act": nc.scalar,
                    "gp": nc.gpsimd, "sy": nc.sync}

    def _wait1(self, waiter, mark):
        if mark is None:
            return
        if mark[0] == "d":
            _, idx, val = mark
            key = (waiter, "d", idx)
            sem = self.dsem[idx]
        else:
            name, val = mark
            key = (waiter, name)
            sem = self.sem[name]
        if self.waited.get(key, 0) >= val:
            return
        self.eng[waiter].wait_ge(sem, val)
        self.waited[key] = val

    def waits(self, waiter, marks):
        for m in marks:
            self._wait1(waiter, m)

    def _inc(self, inst, name):
        self.cnt[name] += 1
        inst.then_inc(self.sem[name], 1)
        return (name, self.cnt[name])

    def _pre(self, waiter, reads, writes, marks):
        for b in reads:
            self._wait1(waiter, b.writer)
        for w in writes:
            self._wait1(waiter, w.writer)
            self.waits(waiter, w.readers)
        self.waits(waiter, marks)

    def _post(self, m, reads, writes):
        for b in reads:
            b.readers.append(m)
        for w in writes:
            w.writer = m
            w.readers = []

    def v(self, fn, *args, marks=(), reads=(), writes=(), **kw):
        self._wait1("dve", ("dve", self.cnt["dve"]))
        self._pre("dve", reads, writes, marks)
        inst = getattr(self.nc.vector, fn)(*args, **kw)
        m = self._inc(inst, "dve")
        self._post(m, reads, writes)
        return m

    def a(self, fn, *args, marks=(), reads=(), writes=(), **kw):
        self._wait1("act", ("act", self.cnt["act"]))
        self._pre("act", reads, writes, marks)
        inst = getattr(self.nc.scalar, fn)(*args, **kw)
        m = self._inc(inst, "act")
        self._post(m, reads, writes)
        return m

    def pe_mm(self, bank, lhsT, rhs, start, stop, rbufs=(), marks=(),
              inc=False, out=None):
        if start:
            self.waits("pe", bank.readers)
            bank.readers = []
        for b in rbufs:
            self._wait1("pe", b.writer)
        bank.pending.extend(rbufs)
        self.waits("pe", marks)
        inst = self.nc.tensor.matmul(out if out is not None else bank.t,
                                     lhsT, rhs, start=start, stop=stop)
        if stop or inc:
            m = self._inc(inst, "pe")
            bank.writer = m
            for b in bank.pending:
                b.readers.append(m)
            bank.pending = []
            return m
        return None

    def pe_tr(self, bank, out_ap, in_buf, in_ap, ident_buf, ident,
              marks=()):
        self.waits("pe", bank.readers)
        bank.readers = []
        for b in (in_buf, ident_buf):
            if b is not None:
                self._wait1("pe", b.writer)
        self.waits("pe", marks)
        inst = self.nc.tensor.transpose(out_ap, in_ap, ident)
        m = self._inc(inst, "pe")
        bank.writer = m
        for b in (in_buf, ident_buf):
            if b is not None:
                b.readers.append(m)
        return m

    def dma(self, q, dst, src, reads=(), writes=(), marks=()):
        eng = self.eng[q]
        for b in reads:
            mk = b.writer
            self._dwait(q, eng, mk)
        for w in writes:
            self._dwait(q, eng, w.writer)
            for mk in w.readers:
                self._dwait(q, eng, mk)
        for mk in marks:
            self._dwait(q, eng, mk)
        pool = self.DPOOL[q]
        r = self.drot_q.setdefault(q, 0)
        idx = pool[r]
        self.drot_q[q] = (r + 1) % len(pool)
        if self.dcnt[idx]:
            self._dwait(q, eng, ("d", idx, self.dcnt[idx]))
        self.dcnt[idx] += 16
        eng.dma_start(dst, src).then_inc(self.dsem[idx], 16)
        m = ("d", idx, self.dcnt[idx])
        for b in reads:
            b.readers.append(m)
        for w in writes:
            w.writer = m
            w.readers = []
        return m

    def _dwait(self, q, eng, mk):
        if mk is None:
            return
        if mk[0] == "d":
            _, idx, val = mk
            key = (q, "d", idx)
            sem = self.dsem[idx]
        else:
            name, val = mk
            key = (q, name)
            sem = self.sem[name]
        if self.waited.get(key, 0) < val:
            eng.wait_ge(sem, val)
            self.waited[key] = val

    def final_dma_wait(self):
        for idx in range(self.NDMA):
            if self.dcnt[idx]:
                self.nc.sync.wait_ge(self.dsem[idx], self.dcnt[idx])

    def alias(self, dst_bufs, src_bufs):
        marks = []
        for s in src_bufs:
            if s.writer is not None:
                marks.append(s.writer)
            marks.extend(s.readers)
        for d in dst_bufs:
            d.writer = None
            d.readers = list(marks)


class BankPool:
    def __init__(self, banks):
        self.banks = banks
        self.i = 0

    def take(self):
        b = self.banks[self.i]
        self.i = (self.i + 1) % len(self.banks)
        return b


class Ring:

    def __init__(self, pk, views):
        self.pk = pk
        self.bufs = [Buf(v, f"slot{i}") for i, v in enumerate(views)]
        self.i = 0

    def load(self, src, kc, cols):
        b = self.bufs[self.i]
        self.i = (self.i + 1) % len(self.bufs)
        view = b.t[:, 0:kc * cols].rearrange("p (k c) -> p k c", c=cols)
        self.pk.dma("gp", view, src, writes=[b])
        return {"view": view, "buf": b}


def _build(upto="full", dump=None, ffn_lim=99):
    nc = bass.Bass("TRN2", target_bir_lowering=False)
    stack = contextlib.ExitStack()

    XIN = nc.dram_tensor("XIN", [128, KC, S], F32, kind="ExternalInput").ap()
    CC = nc.dram_tensor("CC", [128, 16], F32, kind="ExternalInput").ap()
    CONSTS = nc.dram_tensor("CONSTS", [128, 256], F16, kind="ExternalInput").ap()
    SEL = nc.dram_tensor("SEL", [8, 1024], F16, kind="ExternalInput").ap()
    WQ = nc.dram_tensor("WQ", [16, 128, 16, 128], F32, kind="ExternalInput").ap()
    WK = nc.dram_tensor("WK", [16, 128, 16, 128], F32, kind="ExternalInput").ap()
    WV = nc.dram_tensor("WV", [16, 128, 16, 128], F32, kind="ExternalInput").ap()
    WO = nc.dram_tensor("WO", [16, 128, 16, 128], F32, kind="ExternalInput").ap()
    WMOD = nc.dram_tensor("WMOD", [12, 16, 128, 1024], F32, kind="ExternalInput").ap()
    GUP = nc.dram_tensor("GUP", [8, 22, 128, 16, 128], F32, kind="ExternalInput").ap()
    DWN = nc.dram_tensor("DWN", [8, 16, 128, 11, 128], F32, kind="ExternalInput").ap()
    SHGU = nc.dram_tensor("SHGU", [88, 128, 16, 128], F32, kind="ExternalInput").ap()
    SHD = nc.dram_tensor("SHD", [16, 4, 128, 11, 128], F32, kind="ExternalInput").ap()
    RT = nc.dram_tensor("RT", [128, 128], F32, kind="ExternalInput").ap()
    SGW = nc.dram_tensor("SGW", [128, 16], F32, kind="ExternalInput").ap()
    OUT = nc.dram_tensor("OUT", [128, KC, S], F32, kind="ExternalOutput").ap()
    XSP = nc.dram_tensor("XSP", [128, KC, S], F16, kind="Internal").ap()
    DBG = None
    if dump is not None:
        DBG = nc.dram_tensor("DBG", [128, KC, S], F32, kind="ExternalOutput").ap()

    xar = nc.alloc_sbuf_tensor("xar", [128, 65536], I8).ap()
    r1 = nc.alloc_sbuf_tensor("r1", [128, 65536], I8).ap()
    r2 = nc.alloc_sbuf_tensor("r2", [128, 32768], I8).ap()
    wring_t = nc.alloc_sbuf_tensor("wring", [128, 4 * 4096], I8).ap()

    def carve(region, off, dtype, *shape):
        n = int(np.prod(shape)) * (2 if dtype == F16 else 4)
        ap = region[:, off:off + n].bitcast(dtype)
        if len(shape) > 1:
            ap = ap.rearrange("p (a b) -> p a b", b=shape[-1])
        return ap

    x3 = carve(xar, 0, F16, KC, S)
    h1t = carve(r1, 0, F16, KC, 1024)
    qt = carve(r1, 32768, F16, KC, 1024)
    kv3 = carve(r2, 0, F16, KC, 1024)
    acc3 = carve(xar, 0, F16, KC, 1024)
    hidt = [carve(xar, 32768, F16, IC, 1024), carve(r2, 0, F16, IC, 1024)]
    sgt = carve(xar, 55296, F16, 1024)
    cbct = carve(xar, 57344, F16, 1024)
    sbct = carve(xar, 59392, F16, 1024)
    h23 = carve(r1, 0, F16, KC, S)

    def sb(name, shape, dtype):
        return Buf(nc.alloc_sbuf_tensor(name, shape, dtype).ap(), name)

    xB = [Buf(x3, "x3h0"), Buf(x3, "x3h1")]
    h1B = Buf(h1t, "h1t")
    qB = Buf(qt, "qt")
    kvB = Buf(kv3, "kv3")
    sqB = [Buf(kv3[:, :, 0:512], "sq0"), Buf(kv3[:, :, 512:1024], "sq1")]
    accB = Buf(acc3, "acc3")
    hidB = [Buf(hidt[0], "hid0"), Buf(hidt[1], "hid1")]
    sgB = [Buf(sgt[:, 0:512], "sgt0"), Buf(sgt[:, 512:1024], "sgt1")]
    cbB = Buf(cbct, "cbc")
    sbB = Buf(sbct, "sbc")
    h2B = [Buf(h23, "h23a"), Buf(h23, "h23b")]

    consB = sb("cons", [128, 256], F16)
    ident = consB.t[:, 0:128]
    ones1 = consB.t[:, 128:256]
    crawB = sb("c_raw", [128, 16], F32)
    silucB = sb("silu_c", [128, 16], F16)
    modvB = sb("modv", [128, 96], F32)
    modv = modvB.t
    carryB = sb("carry", [128, 16], F32)
    lnAB = sb("lnA", [128, 512], F16)
    lnBB = sb("lnB", [128, 512], F16)
    lnTB = [sb("lnT0", [128, 512], F16), sb("lnT1", [128, 512], F16)]
    lnrow_t = nc.alloc_sbuf_tensor("lnrow", [1, 2, 512], F16).ap()
    lnrB = [Buf(lnrow_t[:, 0, :], "lnr0"), Buf(lnrow_t[:, 1, :], "lnr1")]
    lnFB = sb("lnF", [128, 512], F32)
    stg_t = nc.alloc_sbuf_tensor("stg", [128, 2, 512], F16).ap()
    stg2 = stg_t[:, 0:2, :]
    stgB = [Buf(stg_t[:, 0, :], "stg0"), Buf(stg_t[:, 1, :], "stg1")]
    mrowB = sb("mrow", [1, 512], F16)
    lgB = sb("lg", [8, 512], F16)
    rtB = sb("rt8", [128, 8], F32)
    rpB = sb("rp8", [128, 8], F32)
    rmB = sb("rm8", [128, 8], F32)
    rkB = sb("rk8", [128, 8], F32)
    cmbB = sb("cmb16", [128, 8], F16)
    m1B = sb("m1c", [128, 1], F32)
    m2B = sb("m2c", [128, 1], F32)
    negmB = sb("negm", [128, 1], F32)
    denB = sb("denc", [128, 1], F32)
    crowB = sb("crow", [8, 1024], F16)
    srowB = sb("srow", [1, 1024], F16)
    selB = sb("sel", [8, 1024], F16)
    rtwB = sb("rt_w", [128, 128], F16)
    sgwB = sb("sg_w", [128, 16], F16)
    x1rB = [sb("x1r0", [128, 1024], F16), sb("x1r1", [128, 1024], F16)]
    orB = [sb(f"oring{i}", [128, 512], F32) for i in range(2)]
    tfinB = sb("tfin", [128, 512], F32)

    psb = [Buf(stack.enter_context(
        nc.psum_tensor(f"ps{i}", [128, 512], F32)).ap(), f"ps{i}")
        for i in range(7)]
    pstr_t = stack.enter_context(nc.psum_tensor("pstr", [128, 1024], F16)).ap()
    pstrA = Buf(pstr_t[:, 0:512], "pstrA")
    pstrB = pstrA

    pk = PK(nc, stack)
    pool = BankPool(psb)
    ring = Ring(pk, [wring_t[:, s * 4096:(s + 1) * 4096].bitcast(F16)
                     for s in range(4)])

    pk.dma("sy", consB.t, CONSTS, writes=[consB])
    pk.dma("sy", crawB.t, CC, writes=[crawB])
    pk.dma("sy", selB.t, SEL, writes=[selB])
    pk.dma("gp", rtwB.t, RT, writes=[rtwB])
    pk.dma("gp", sgwB.t, SGW, writes=[sgwB])
    pk.a("activation", silucB.t, crawB.t, AF.Silu,
         reads=[crawB], writes=[silucB])
    pk.v("memset", carryB.t, 0.0, writes=[carryB])

    def emit_mod(fs_range):
        for fs in fs_range:
            v = fs // 2
            cbase = (fs % 2) * 8
            bA, bB = pool.take(), pool.take()
            for k in range(KC):
                p = ring.load(WMOD[fs, k], 1, 1024)
                for h, bank in ((0, bA), (1, bB)):
                    pk.pe_mm(bank, silucB.t[:, k:k + 1],
                             p["view"][:, 0, h * 512:(h + 1) * 512],
                             start=(k == 0), stop=(k == KC - 1),
                             rbufs=[p["buf"], silucB],
                             out=bank.t[0:1, :], inc=(h == 1))
            for h, bank in ((0, bA), (1, bB)):
                pk.a("copy", mrowB.t, bank.t[0:1, :], reads=[bank],
                     writes=[mrowB])
                for j in range(4):
                    pk.pe_tr(pstrA, pstrA.t[:, 0:1], mrowB,
                             mrowB.t[0:1, j * 128:(j + 1) * 128],
                             consB, ident[0:1, 0:1])
                    col = v * 16 + cbase + h * 4 + j
                    pk.a("copy", modv[:, col:col + 1], pstrA.t[:, 0:1],
                         reads=[pstrA], writes=[modvB])

    with nc.named_scope("modA"):
        emit_mod(range(2))
        pk.dma("gp", x3, XIN, writes=xB)
        emit_mod(range(2, 4))
    if upto == "modraw":
        emit_mod(range(4, 12))

    def emit_gates(vs):
        for vname in vs:
            c0 = MODC[vname]
            sl = modv[:, c0:c0 + 16]
            if vname in ("ca", "cb"):
                pk.v("tensor_single_scalar", sl, sl, 1.0, ALU.add,
                     reads=[modvB], writes=[modvB])
            else:
                sgn = lnFB.t[:, 0:16]
                mag = lnFB.t[:, 16:32]
                pk.v("tensor_single_scalar", sgn, sl, 0.0, ALU.is_ge,
                     reads=[modvB], writes=[lnFB])
                pk.v("tensor_scalar", sgn, sgn, 2.0, -1.0,
                     op0=ALU.mult, op1=ALU.add,
                     reads=[lnFB], writes=[lnFB])
                pk.a("activation", mag, sl, AF.Abs,
                     reads=[modvB], writes=[lnFB])
                pk.v("tensor_single_scalar", mag, mag, 0.25, ALU.max,
                     reads=[lnFB], writes=[lnFB])
                pk.v("tensor_mul", sl, sgn, mag,
                     reads=[lnFB], writes=[modvB])

    if upto != "modraw":
        emit_gates(["ca"])

    def emit_ln(srcB, src3, dstB, dst3, ca_col, sa_col, s_lo, s_n, d_lo=None):
        if d_lo is None:
            d_lo = s_lo
        subs = list(range(s_n // 512))
        state = {}

        def squares(sub):
            s0 = s_lo + sub * 512
            sB_half = srcB[s0 // 1024] if isinstance(srcB, list) else srcB
            sq = sqB[sub % 2]
            for k in range(0, KC, 2):
                xs2 = src3[:, k:k + 2, s0:s0 + 512]
                pk.v("tensor_mul", sq.t[:, k:k + 2, :], xs2, xs2,
                     reads=[sB_half], writes=[sq])

        def stats(sub):
            s0 = s_lo + sub * 512
            sB_half = srcB[s0 // 1024] if isinstance(srcB, list) else srcB
            sq = sqB[sub % 2]
            bA, bB = pool.take(), pool.take()
            for k in range(KC):
                xs = src3[:, k, s0:s0 + 512]
                pk.pe_mm(bA, ones1, xs, start=(k == 0),
                         stop=(k == KC - 1), rbufs=[consB, sB_half])
                pk.pe_mm(bB, ones1, sq.t[:, k, :], start=(k == 0),
                         stop=(k == KC - 1), rbufs=[consB, sq])
            state[sub] = (bA, bB, s0, s_lo and 0)

        def tail(sub):
            bA, bB, s0, _ = state.pop(sub)
            d0 = d_lo + sub * 512
            sB_half = srcB[s0 // 1024] if isinstance(srcB, list) else srcB
            dB_half = dstB[(d_lo + sub * 512) // 1024] \
                if isinstance(dstB, list) else dstB
            r0, r1 = lnrB
            r2 = mrowB
            r3s = srowB.t[0:1, 0:512]
            pk.a("activation", r0.t, bA.t[0:1, :], AF.Identity,
                 scale=1.0 / 2048.0, reads=[bA], writes=[r0])
            pk.a("activation", r1.t, bB.t[0:1, :], AF.Identity,
                 scale=1.0 / 2048.0, reads=[bB], writes=[r1])
            pk.v("tensor_mul", r2.t, r0.t, r0.t, reads=[r0], writes=[mrowB])
            pk.v("tensor_tensor", r1.t, r1.t, r2.t, op=ALU.subtract,
                 reads=[r1, mrowB], writes=[r1])
            pk.v("tensor_single_scalar", r1.t, r1.t, EPS, ALU.add,
                 reads=[r1], writes=[r1])
            with nc.allow_low_precision(reason="f16 LN row scratch, 5e-4 rel"):
                pk.v("reciprocal", r2.t, r1.t, reads=[r1], writes=[mrowB])
            pk.a("sqrt", r3s, r2.t, reads=[mrowB], writes=[srowB])
            pk.v("scalar_tensor_tensor", r1.t, r0.t, -1.0, r3s,
                 op0=ALU.mult, op1=ALU.mult,
                 reads=[r0, srowB], writes=[r1])
            bc1, bc2 = pool.take(), pool.take()
            pk.pe_mm(bc1, ones1[0:1, :], r3s, start=True, stop=True,
                     rbufs=[consB, srowB])
            pk.pe_mm(bc2, ones1[0:1, :], r1.t, start=True, stop=True,
                     rbufs=[consB, r1])
            pk.v("tensor_copy", lnAB.t, bc1.t, reads=[bc1], writes=[lnAB])
            pk.v("tensor_copy", lnBB.t, bc2.t, reads=[bc2], writes=[lnBB])
            for k in range(KC):
                tB = lnTB[k % 2]
                pk.v("tensor_mul", tB.t, src3[:, k, s0:s0 + 512], lnAB.t,
                     reads=[sB_half, lnAB], writes=[tB])
                pk.v("tensor_tensor", tB.t, tB.t, lnBB.t, op=ALU.add,
                     reads=[tB, lnBB], writes=[tB])
                pk.a("activation", dst3[:, k, d0:d0 + 512], tB.t,
                     AF.Identity,
                     bias=modv[:, sa_col + k:sa_col + k + 1],
                     scale=modv[:, ca_col + k:ca_col + k + 1],
                     reads=[tB, modvB], writes=[dB_half])

        squares(subs[0])
        for sub in subs:
            stats(sub)
            if sub + 1 in subs:
                squares(sub + 1)
            if sub - 1 in subs:
                tail(sub - 1)
        tail(subs[-1])

    def emit_qkv(t, fuse_scan=True):
        for f in range(16):
            pq = ring.load(WQ[f], KC, 128)
            pkk = ring.load(WK[f], KC, 128)
            pv = ring.load(WV[f], KC, 128)
            bq = [pool.take(), pool.take()]
            bk = [pool.take(), pool.take()]
            bv = [pool.take(), pool.take()]
            for p, bb in ((pq, bq), (pkk, bk), (pv, bv)):
                for st in range(2):
                    for k in range(KC):
                        pk.pe_mm(bb[st], p["view"][:, k, :],
                                 h1t[:, k, st * 512:(st + 1) * 512],
                                 start=(k == 0), stop=(k == KC - 1),
                                 rbufs=[p["buf"], h1B])
            for st in range(2):
                pk.a("copy", qt[:, f, st * 512:(st + 1) * 512], bq[st].t,
                     reads=[bq[st]], writes=[qB])
                pk.a("copy", stgB[st].t, bk[st].t, reads=[bk[st]],
                     writes=[stgB[st]])
                pk.v("tensor_mul", kv3[:, f, st * 512:(st + 1) * 512],
                     stgB[st].t, bv[st].t,
                     reads=[stgB[st], bv[st]], writes=[kvB])
            if fuse_scan:
                k = f
                pk.v("tensor_tensor_scan", kv3[:, k, :], kv3[:, k, :],
                     ones1[:, 0:1].broadcast_to([128, 1024]),
                     carryB.t[:, k:k + 1], op0=ALU.add, op1=ALU.mult,
                     reads=[kvB, consB, carryB], writes=[kvB])
                pk.v("tensor_copy", carryB.t[:, k:k + 1],
                     kv3[:, k, 1023:1024], reads=[kvB], writes=[carryB])
                pk.v("scalar_tensor_tensor", qt[:, k, :], qt[:, k, :],
                     DENOM, kv3[:, k, :], op0=ALU.mult, op1=ALU.mult,
                     reads=[qB, kvB], writes=[qB])

    def emit_scan_vhat():
        for k in range(KC):
            pk.v("tensor_tensor_scan", kv3[:, k, :], kv3[:, k, :],
                 ones1[:, 0:1].broadcast_to([128, 1024]),
                 carryB.t[:, k:k + 1], op0=ALU.add, op1=ALU.mult,
                 reads=[kvB, consB, carryB], writes=[kvB])
            pk.v("tensor_copy", carryB.t[:, k:k + 1], kv3[:, k, 1023:1024],
                 reads=[kvB], writes=[carryB])
            pk.v("scalar_tensor_tensor", qt[:, k, :], qt[:, k, :],
                 DENOM, kv3[:, k, :], op0=ALU.mult, op1=ALU.mult,
                 reads=[qB, kvB], writes=[qB])

    def emit_wo(t):
        s0 = t * 1024
        for f in range(16):
            po = ring.load(WO[f], KC, 128)
            for st in range(2):
                b = pool.take()
                for k in range(KC):
                    pk.pe_mm(b, po["view"][:, k, :],
                             qt[:, k, st * 512:(st + 1) * 512],
                             start=(k == 0), stop=(k == KC - 1),
                             rbufs=[po["buf"], qB])
                ga_c = MODC["ga"] + f
                xs = x3[:, f, s0 + st * 512:s0 + (st + 1) * 512]
                pk.v("scalar_tensor_tensor", xs, b.t,
                     modv[:, ga_c:ga_c + 1], xs,
                     op0=ALU.mult, op1=ALU.add,
                     reads=[b, modvB], writes=[xB[t]])

    m_spill = None
    if upto in ("attn", "ln2", "full"):
        with nc.named_scope("ln1a"):
            pk.alias(sqB, [kvB])
            emit_ln(xB, x3, h1B, h1t, MODC["ca"], MODC["sa"], 0, 1024)
        pk.alias([kvB], sqB)
        with nc.named_scope("qkv0"):
            emit_qkv(0)
        with nc.named_scope("modA2"):
            emit_mod(range(4, 6))
            emit_gates(["ga"])
        with nc.named_scope("ln1b"):
            pk.alias(sqB, [kvB])
            emit_ln(xB, x3, h1B, h1t, MODC["ca"], MODC["sa"], 1024, 1024,
                    d_lo=0)
        pk.alias([kvB], sqB)
        with nc.named_scope("wo0"):
            emit_wo(0)
        with nc.named_scope("qkv1"):
            emit_qkv(1)
        with nc.named_scope("modB"):
            emit_mod(range(6, 12))
            emit_gates(["cb", "gb"])
        with nc.named_scope("wo1"):
            emit_wo(1)

    if upto in ("ln2", "full"):
        pk.alias(h2B, [h1B, qB])
        with nc.named_scope("ln2"):
            pk.alias(sqB, [kvB])
            emit_ln(xB, x3, h2B, h23, MODC["cb"], MODC["sb"], 0, 2048)
        m_spill = pk.dma("sy", XSP, x3, reads=xB)
        pk.alias([accB, hidB[0], sgB[0], sgB[1], cbB, sbB], xB)
        pk.alias([hidB[1]], [kvB] + sqB)

    def gu_src(pe_i, j):
        if pe_i < 4:
            return SHGU[22 * pe_i + j]
        return GUP[pe_i - 4, j]

    def dn_src(pe_i, f):
        if pe_i < 4:
            return SHD[f, pe_i]
        return DWN[pe_i - 4, f]

    def emit_ffn_half(hf):
        s0 = hf * 1024
        for st in range(2 if ffn_lim >= -3 else 0):
            b = pool.take()
            for k in range(KC):
                pk.pe_mm(b, rtwB.t[:, k * 8:(k + 1) * 8],
                         h23[:, k, s0 + st * 512:s0 + (st + 1) * 512],
                         start=(k == 0), stop=(k == KC - 1),
                         rbufs=[rtwB, h2B[hf]], out=b.t[0:8, :])
            pk.a("copy", lgB.t, b.t[0:8, :], reads=[b], writes=[lgB])
            for j in range(4 if ffn_lim >= -2 else 0):
                pk.pe_tr(pstrA, pstrA.t[0:128, 0:8], lgB,
                         lgB.t[0:8, j * 128:(j + 1) * 128],
                         consB, ident[0:8, 0:8])
                pk.v("tensor_copy", rtB.t, pstrA.t[0:128, 0:8],
                     reads=[pstrA], writes=[rtB])
                pk.v("reduce_max", m1B.t, rtB.t, axis=AX,
                     reads=[rtB], writes=[m1B])
                pk.v("tensor_single_scalar", negmB.t, m1B.t, -1.0,
                     ALU.mult, reads=[m1B], writes=[negmB])
                pk.v("tensor_single_scalar", rkB.t, rtB.t, m1B.t,
                     ALU.is_ge, reads=[rtB, m1B], writes=[rkB])
                pk.v("scalar_tensor_tensor", rpB.t, rkB.t, -1e30, rtB.t,
                     op0=ALU.mult, op1=ALU.add,
                     reads=[rkB, rtB], writes=[rpB])
                pk.v("reduce_max", m2B.t, rpB.t, axis=AX,
                     reads=[rpB], writes=[m2B])
                pk.a("activation", rpB.t, rtB.t, AF.Exp, bias=negmB.t,
                     reads=[rtB, negmB], writes=[rpB])
                pk.v("tensor_single_scalar", rkB.t, rtB.t, m2B.t,
                     ALU.is_ge, reads=[rtB, m2B], writes=[rkB])
                pk.v("tensor_mul", rmB.t, rpB.t, rkB.t,
                     reads=[rpB, rkB], writes=[rmB])
                pk.v("reduce_sum", denB.t, rmB.t, axis=AX,
                     reads=[rmB], writes=[denB])
                pk.v("reciprocal", denB.t, denB.t,
                     reads=[denB], writes=[denB])
                pk.v("tensor_single_scalar", cmbB.t, rmB.t, denB.t,
                     ALU.mult, reads=[rmB, denB], writes=[cmbB])
                if ffn_lim >= -1:
                    pk.pe_tr(pstrB, pstr_t[0:8, 512:640], cmbB, cmbB.t,
                             consB, ident)
                    pk.a("copy",
                         crowB.t[0:8, st * 512 + j * 128:st * 512 + (j + 1) * 128],
                         pstr_t[0:8, 512:640], reads=[pstrB], writes=[crowB])
        for st in range(2 if ffn_lim >= 0 else 0):
            b = pool.take()
            for k in range(KC):
                pk.pe_mm(b, sgwB.t[:, k:k + 1],
                         h23[:, k, s0 + st * 512:s0 + (st + 1) * 512],
                         start=(k == 0), stop=(k == KC - 1),
                         rbufs=[sgwB, h2B[hf]], out=b.t[0:1, :])
            pk.a("activation", srowB.t[0:1, st * 512:(st + 1) * 512],
                 b.t[0:1, :], AF.Sigmoid, reads=[b], writes=[srowB])
            b2 = pool.take()
            pk.pe_mm(b2, ones1[0:1, :],
                     srowB.t[0:1, st * 512:(st + 1) * 512],
                     start=True, stop=True, rbufs=[consB, srowB])
            pk.v("tensor_copy", sbct[:, st * 512:(st + 1) * 512], b2.t,
                 reads=[b2], writes=[sbB])
        for pe_i in range(min(NPE, ffn_lim)):
            scope = stack.enter_context(nc.named_scope(f"pe{hf}_{pe_i}")) if False else None
            slot = pe_i % 2
            if pe_i < 4:
                scB, sct = sbB, sbct
            else:
                e = pe_i - 4
                for st in range(2):
                    b2 = pool.take()
                    pk.pe_mm(b2, selB.t[:, e * 128:(e + 1) * 128],
                             crowB.t[0:8, st * 512:(st + 1) * 512],
                             start=True, stop=True, rbufs=[selB, crowB])
                    pk.v("tensor_copy", cbct[:, st * 512:(st + 1) * 512],
                         b2.t, reads=[b2], writes=[cbB])
                scB, sct = cbB, cbct
            gu_scope = nc.named_scope(f"gu{hf}_{pe_i}")
            gu_scope.__enter__()
            for i in range(IC):
                pg = ring.load(gu_src(pe_i, 2 * i), KC, 128)
                pu = ring.load(gu_src(pe_i, 2 * i + 1), KC, 128)
                bg = [pool.take(), pool.take()]
                bu = [pool.take(), pool.take()]
                for p, bb in ((pg, bg), (pu, bu)):
                    for st in range(2):
                        for k in range(KC):
                            pk.pe_mm(bb[st], p["view"][:, k, :],
                                     h23[:, k, s0 + st * 512:s0 + (st + 1) * 512],
                                     start=(k == 0), stop=(k == KC - 1),
                                     rbufs=[p["buf"], h2B[hf]])
                for st in range(2):
                    pk.a("activation", sgt[:, st * 512:(st + 1) * 512],
                         bg[st].t, AF.Silu, reads=[bg[st]], writes=[sgB[st]])
                    hv = hidt[slot][:, i, st * 512:(st + 1) * 512]
                    pk.v("tensor_mul", hv, bu[st].t,
                         sct[:, st * 512:(st + 1) * 512],
                         reads=[bu[st], scB], writes=[hidB[slot]])
                    pk.v("tensor_mul", hv, hv,
                         sgt[:, st * 512:(st + 1) * 512],
                         reads=[hidB[slot], sgB[st]], writes=[hidB[slot]])
            gu_scope.__exit__(None, None, None)
            if pe_i % 2 == 1:
                r = pe_i // 2
                peA, peB = pe_i - 1, pe_i
                dn_scope = nc.named_scope(f"dn{hf}_{r}")
                dn_scope.__enter__()
                for f in range(16):
                    pa = ring.load(dn_src(peA, f), IC, 128)
                    pb = ring.load(dn_src(peB, f), IC, 128)
                    if r == 5:
                        pk.dma("sy", x1rB[f % 2].t,
                               XSP[:, f, s0:s0 + 1024],
                               marks=[m_spill], writes=[x1rB[f % 2]])
                    for st in range(2):
                        b = pool.take()
                        for k in range(IC):
                            pk.pe_mm(b, pa["view"][:, k, :],
                                     hidt[peA % 2][:, k, st * 512:(st + 1) * 512],
                                     start=(k == 0), stop=False,
                                     rbufs=[pa["buf"], hidB[peA % 2]])
                        for k in range(IC):
                            pk.pe_mm(b, pb["view"][:, k, :],
                                     hidt[peB % 2][:, k, st * 512:(st + 1) * 512],
                                     start=False, stop=(k == IC - 1),
                                     rbufs=[pb["buf"], hidB[peB % 2]])
                        a_sl = acc3[:, f, st * 512:(st + 1) * 512]
                        if r == 0:
                            pk.a("copy", a_sl, b.t, reads=[b], writes=[accB])
                        elif r < 5:
                            pk.v("tensor_tensor", a_sl, b.t, a_sl,
                                 op=ALU.add, reads=[b, accB], writes=[accB])
                        else:
                            gb_c = MODC["gb"] + f
                            pk.v("tensor_tensor", tfinB.t, b.t, a_sl,
                                 op=ALU.add, reads=[b, accB], writes=[tfinB])
                            sidx = (f * 2 + st) % 2
                            pk.v("scalar_tensor_tensor", orB[sidx].t,
                                 tfinB.t, modv[:, gb_c:gb_c + 1],
                                 x1rB[f % 2].t[:, st * 512:(st + 1) * 512],
                                 op0=ALU.mult, op1=ALU.add,
                                 reads=[tfinB, modvB, x1rB[f % 2]],
                                 writes=[orB[sidx]])
                            pk.dma("sy",
                                   OUT[:, f, s0 + st * 512:s0 + (st + 1) * 512],
                                   orB[sidx].t, reads=[orB[sidx]])
                _dn_last = (f == 15)
                if _dn_last:
                    dn_scope.__exit__(None, None, None)

    if upto == "full":
        for hf in range(2 if ffn_lim >= 99 else 1):
            with nc.named_scope(f"ffn{hf}"):
                emit_ffn_half(hf)

        if ffn_lim >= 99:
            pk.final_dma_wait()
        else:
            zm = pk.v("memset", lnTB[0].t, 0.0,
                      marks=[("dve", pk.cnt["dve"]), ("act", pk.cnt["act"]),
                             ("pe", pk.cnt["pe"])], writes=[lnTB[0]])
            for kk in range(KC):
                for st4 in range(4):
                    pk.dma("sy", OUT[:, kk, st4 * 512:(st4 + 1) * 512],
                           lnTB[0].t, reads=[lnTB[0]])
            pk.final_dma_wait()

    if dump is not None:
        allm = [("dve", pk.cnt["dve"]), ("act", pk.cnt["act"]),
                ("pe", pk.cnt["pe"])]
        if dump == "mod":
            pk.dma("sy", DBG[:, 0, 0:96], modv, marks=allm)
        elif dump == "x1":
            pk.dma("gp", DBG, x3, marks=allm)
        elif dump == "h1":
            pk.dma("gp", DBG[:, :, 0:1024], h1t, marks=allm)
        elif dump == "kv":
            pk.dma("gp", DBG[:, :, 0:1024], kv3, marks=allm)
        elif dump == "h2":
            pk.dma("gp", DBG, h23, marks=allm)
        pk.final_dma_wait()
        if upto != "full":
            zm = pk.v("memset", lnTB[0].t, 0.0, marks=allm,
                      writes=[lnTB[0]])
            for kk in range(KC):
                for st4 in range(4):
                    pk.dma("sy", OUT[:, kk, st4 * 512:(st4 + 1) * 512],
                           lnTB[0].t, reads=[lnTB[0]])
            pk.final_dma_wait()

    stack.close()
    return nc


_NC_CACHE = {}


def _get_nc(upto="full", dump=None, ffn_lim=99):
    key = (upto, dump, ffn_lim)
    if key not in _NC_CACHE:
        _NC_CACHE[key] = _build(upto, dump, ffn_lim)
    return _NC_CACHE[key]



def _pan2048(W):
    A = np.ascontiguousarray(np.asarray(W, dtype=np.float32).T)
    return np.ascontiguousarray(
        A.reshape(16, 128, 16, 128).transpose(2, 1, 0, 3))


def _prep_inputs(ins):
    x = np.asarray(ins["L0_attn_x"], dtype=np.float32)
    c = np.asarray(ins["L0_c"], dtype=np.float32)
    wmod = np.asarray(ins["L0_wmod"], dtype=np.float32)
    gup = np.asarray(ins["L0_ffn_gate_up_proj"], dtype=np.float32)
    dwn = np.asarray(ins["L0_ffn_down_proj"], dtype=np.float32)
    shg = np.asarray(ins["L0_ffn_sh_gate_w"], dtype=np.float32)
    shu = np.asarray(ins["L0_ffn_sh_up_w"], dtype=np.float32)
    shd = np.asarray(ins["L0_ffn_sh_down_w"], dtype=np.float32)
    rw = np.asarray(ins["L0_ffn_router_weight"], dtype=np.float32)
    sgw = np.asarray(ins["L0_ffn_shared_gate_w"], dtype=np.float32)

    shared = {
        "WQ": _pan2048(ins["L0_attn_Wq"]),
        "WK": _pan2048(ins["L0_attn_Wk"]),
        "WV": _pan2048(ins["L0_attn_Wv"]),
        "WO": _pan2048(ins["L0_attn_Wo"]),
        "WMOD": np.ascontiguousarray(
            wmod.reshape(16, 128, 12, 1024).transpose(2, 0, 1, 3)),
        "RT": np.ascontiguousarray(
            rw.T.reshape(16, 128, 8).transpose(1, 0, 2).reshape(128, 128)),
        "SGW": np.ascontiguousarray(sgw[0].reshape(16, 128).T),
        "CONSTS": np.concatenate(
            [np.eye(128, dtype=np.float32),
             np.ones((128, 128), np.float32)], axis=1).astype(np.float16),
        "SEL": np.repeat(np.eye(8, dtype=np.float32), 128,
                         axis=1).astype(np.float16),
    }
    idx = []
    for i in range(11):
        idx += [i, 11 + i]
    g8 = []
    for e in range(8):
        A = np.ascontiguousarray(gup[e].T)
        g8.append(A.reshape(16, 128, 22, 128).transpose(2, 1, 0, 3)[idx])
    shared["GUP"] = np.ascontiguousarray(np.stack(g8))
    d8 = []
    for e in range(8):
        A = np.ascontiguousarray(dwn[e].T)
        d8.append(A.reshape(11, 128, 16, 128).transpose(2, 1, 0, 3))
    shared["DWN"] = np.ascontiguousarray(np.stack(d8))
    Hg = np.ascontiguousarray(shg.T).reshape(
        16, 128, 44, 128).transpose(2, 1, 0, 3)
    Hu = np.ascontiguousarray(shu.T).reshape(
        16, 128, 44, 128).transpose(2, 1, 0, 3)
    Hgu = np.empty((88,) + Hg.shape[1:], dtype=np.float32)
    Hgu[0::2] = Hg
    Hgu[1::2] = Hu
    shared["SHGU"] = np.ascontiguousarray(Hgu)
    Ad = np.ascontiguousarray(shd.T)
    shared["SHD"] = np.ascontiguousarray(
        Ad.reshape(4, 11, 128, 16, 128).transpose(3, 0, 2, 1, 4))

    in_maps = []
    for b in range(8):
        m = dict(shared)
        m["XIN"] = np.ascontiguousarray(
            x[b].T.reshape(16, 128, 2048).transpose(1, 0, 2))
        m["CC"] = np.ascontiguousarray(c[b].reshape(16, 128).T)
        in_maps.append(m)
    return in_maps


def kernel(**ins):
    in_maps = _prep_inputs(ins)
    nc = _get_nc()
    res = run_bass_kernel_spmd(nc, in_maps, core_ids=list(range(8)))
    outs = []
    for b in range(8):
        o = np.asarray(res.results[b]["OUT"])
        outs.append(o.transpose(1, 0, 2).reshape(2048, 2048).T)
    return np.ascontiguousarray(np.stack(outs)).astype(np.float32)


# version 11 (final)
# speedup vs baseline: 1.1599x; 1.0132x over previous
import contextlib
import numpy as np

import concourse.bass as bass
import concourse.mybir as mybir
from concourse.bass_utils import run_bass_kernel_spmd

F32 = mybir.dt.float32
F16 = mybir.dt.float16
I8 = mybir.dt.int8
AF = mybir.ActivationFunctionType
ALU = mybir.AluOpType
AX = mybir.AxisListType.X

KC = 16
S = 2048
EPS = 1e-5
DENOM = float(np.sqrt(np.float32(2048.0)) / np.float32(2048.0))
IC = 11
NPE = 12

MODC = {"sa": 0, "ca": 16, "ga": 32, "sb": 48, "cb": 64, "gb": 80}


class Buf:

    def __init__(self, t=None, name=""):
        self.t = t
        self.name = name
        self.writer = None
        self.readers = []
        self.pending = []


class PK:
    NDMA = 24
    DPOOL = {"gp": list(range(16)), "sy": list(range(16, 24))}

    def __init__(self, nc, stack):
        self.nc = nc
        self.sem = {}
        for name in ("pe", "dve", "act"):
            self.sem[name] = stack.enter_context(nc.semaphore(name))
        self.dsem = [stack.enter_context(nc.semaphore(f"dm{i}"))
                     for i in range(self.NDMA)]
        self.cnt = {"pe": 0, "dve": 0, "act": 0}
        self.dcnt = [0] * self.NDMA
        self.drot_q = {}
        self.waited = {}
        self.eng = {"pe": nc.tensor, "dve": nc.vector, "act": nc.scalar,
                    "gp": nc.gpsimd, "sy": nc.sync}

    def _wait1(self, waiter, mark):
        if mark is None:
            return
        if mark[0] == "d":
            _, idx, val = mark
            key = (waiter, "d", idx)
            sem = self.dsem[idx]
        else:
            name, val = mark
            key = (waiter, name)
            sem = self.sem[name]
        if self.waited.get(key, 0) >= val:
            return
        self.eng[waiter].wait_ge(sem, val)
        self.waited[key] = val

    def waits(self, waiter, marks):
        for m in marks:
            self._wait1(waiter, m)

    def _inc(self, inst, name):
        self.cnt[name] += 1
        inst.then_inc(self.sem[name], 1)
        return (name, self.cnt[name])

    def _pre(self, waiter, reads, writes, marks):
        for b in reads:
            self._wait1(waiter, b.writer)
        for w in writes:
            self._wait1(waiter, w.writer)
            self.waits(waiter, w.readers)
        self.waits(waiter, marks)

    def _post(self, m, reads, writes):
        for b in reads:
            b.readers.append(m)
        for w in writes:
            w.writer = m
            w.readers = []

    def v(self, fn, *args, marks=(), reads=(), writes=(), **kw):
        self._wait1("dve", ("dve", self.cnt["dve"]))
        self._pre("dve", reads, writes, marks)
        inst = getattr(self.nc.vector, fn)(*args, **kw)
        m = self._inc(inst, "dve")
        self._post(m, reads, writes)
        return m

    def a(self, fn, *args, marks=(), reads=(), writes=(), **kw):
        self._wait1("act", ("act", self.cnt["act"]))
        self._pre("act", reads, writes, marks)
        inst = getattr(self.nc.scalar, fn)(*args, **kw)
        m = self._inc(inst, "act")
        self._post(m, reads, writes)
        return m

    def pe_mm(self, bank, lhsT, rhs, start, stop, rbufs=(), marks=(),
              inc=False, out=None):
        if start:
            self.waits("pe", bank.readers)
            bank.readers = []
        for b in rbufs:
            self._wait1("pe", b.writer)
        bank.pending.extend(rbufs)
        self.waits("pe", marks)
        inst = self.nc.tensor.matmul(out if out is not None else bank.t,
                                     lhsT, rhs, start=start, stop=stop)
        if stop or inc:
            m = self._inc(inst, "pe")
            bank.writer = m
            for b in bank.pending:
                b.readers.append(m)
            bank.pending = []
            return m
        return None

    def pe_tr(self, bank, out_ap, in_buf, in_ap, ident_buf, ident,
              marks=()):
        self.waits("pe", bank.readers)
        bank.readers = []
        for b in (in_buf, ident_buf):
            if b is not None:
                self._wait1("pe", b.writer)
        self.waits("pe", marks)
        inst = self.nc.tensor.transpose(out_ap, in_ap, ident)
        m = self._inc(inst, "pe")
        bank.writer = m
        for b in (in_buf, ident_buf):
            if b is not None:
                b.readers.append(m)
        return m

    def dma(self, q, dst, src, reads=(), writes=(), marks=()):
        eng = self.eng[q]
        for b in reads:
            mk = b.writer
            self._dwait(q, eng, mk)
        for w in writes:
            self._dwait(q, eng, w.writer)
            for mk in w.readers:
                self._dwait(q, eng, mk)
        for mk in marks:
            self._dwait(q, eng, mk)
        pool = self.DPOOL[q]
        r = self.drot_q.setdefault(q, 0)
        idx = pool[r]
        self.drot_q[q] = (r + 1) % len(pool)
        if self.dcnt[idx]:
            self._dwait(q, eng, ("d", idx, self.dcnt[idx]))
        self.dcnt[idx] += 16
        eng.dma_start(dst, src).then_inc(self.dsem[idx], 16)
        m = ("d", idx, self.dcnt[idx])
        for b in reads:
            b.readers.append(m)
        for w in writes:
            w.writer = m
            w.readers = []
        return m

    def _dwait(self, q, eng, mk):
        if mk is None:
            return
        if mk[0] == "d":
            _, idx, val = mk
            key = (q, "d", idx)
            sem = self.dsem[idx]
        else:
            name, val = mk
            key = (q, name)
            sem = self.sem[name]
        if self.waited.get(key, 0) < val:
            eng.wait_ge(sem, val)
            self.waited[key] = val

    def final_dma_wait(self):
        for idx in range(self.NDMA):
            if self.dcnt[idx]:
                self.nc.sync.wait_ge(self.dsem[idx], self.dcnt[idx])

    def alias(self, dst_bufs, src_bufs):
        marks = []
        for s in src_bufs:
            if s.writer is not None:
                marks.append(s.writer)
            marks.extend(s.readers)
        for d in dst_bufs:
            d.writer = None
            d.readers = list(marks)


class BankPool:
    def __init__(self, banks):
        self.banks = banks
        self.i = 0

    def take(self):
        b = self.banks[self.i]
        self.i = (self.i + 1) % len(self.banks)
        return b


class Ring:

    def __init__(self, pk, views):
        self.pk = pk
        self.bufs = [Buf(v, f"slot{i}") for i, v in enumerate(views)]
        self.i = 0

    def load(self, src, kc, cols):
        b = self.bufs[self.i]
        self.i = (self.i + 1) % len(self.bufs)
        view = b.t[:, 0:kc * cols].rearrange("p (k c) -> p k c", c=cols)
        self.pk.dma("gp", view, src, writes=[b])
        return {"view": view, "buf": b}


def _build(upto="full", dump=None, ffn_lim=99):
    nc = bass.Bass("TRN2", target_bir_lowering=False)
    stack = contextlib.ExitStack()

    XIN = nc.dram_tensor("XIN", [128, KC, S], F32, kind="ExternalInput").ap()
    CC = nc.dram_tensor("CC", [128, 16], F32, kind="ExternalInput").ap()
    CONSTS = nc.dram_tensor("CONSTS", [128, 256], F16, kind="ExternalInput").ap()
    SEL = nc.dram_tensor("SEL", [8, 1024], F16, kind="ExternalInput").ap()
    WQ = nc.dram_tensor("WQ", [16, 128, 16, 128], F32, kind="ExternalInput").ap()
    WK = nc.dram_tensor("WK", [16, 128, 16, 128], F32, kind="ExternalInput").ap()
    WV = nc.dram_tensor("WV", [16, 128, 16, 128], F32, kind="ExternalInput").ap()
    WO = nc.dram_tensor("WO", [16, 128, 16, 128], F32, kind="ExternalInput").ap()
    WMOD = nc.dram_tensor("WMOD", [12, 16, 128, 1024], F32, kind="ExternalInput").ap()
    GUP = nc.dram_tensor("GUP", [8, 22, 128, 16, 128], F32, kind="ExternalInput").ap()
    DWN = nc.dram_tensor("DWN", [8, 16, 128, 11, 128], F32, kind="ExternalInput").ap()
    SHGU = nc.dram_tensor("SHGU", [88, 128, 16, 128], F32, kind="ExternalInput").ap()
    SHD = nc.dram_tensor("SHD", [16, 4, 128, 11, 128], F32, kind="ExternalInput").ap()
    RT = nc.dram_tensor("RT", [128, 128], F32, kind="ExternalInput").ap()
    SGW = nc.dram_tensor("SGW", [128, 16], F32, kind="ExternalInput").ap()
    OUT = nc.dram_tensor("OUT", [128, KC, S], F32, kind="ExternalOutput").ap()
    XSP = nc.dram_tensor("XSP", [128, KC, S], F16, kind="Internal").ap()
    DBG = None
    if dump is not None:
        DBG = nc.dram_tensor("DBG", [128, KC, S], F32, kind="ExternalOutput").ap()

    xar = nc.alloc_sbuf_tensor("xar", [128, 65536], I8).ap()
    r1 = nc.alloc_sbuf_tensor("r1", [128, 65536], I8).ap()
    r2 = nc.alloc_sbuf_tensor("r2", [128, 32768], I8).ap()
    wring_t = nc.alloc_sbuf_tensor("wring", [128, 4 * 4096], I8).ap()

    def carve(region, off, dtype, *shape):
        n = int(np.prod(shape)) * (2 if dtype == F16 else 4)
        ap = region[:, off:off + n].bitcast(dtype)
        if len(shape) > 1:
            ap = ap.rearrange("p (a b) -> p a b", b=shape[-1])
        return ap

    x3 = carve(xar, 0, F16, KC, S)
    h1t = carve(r1, 0, F16, KC, 1024)
    qt = carve(r1, 32768, F16, KC, 1024)
    kv3 = carve(r2, 0, F16, KC, 1024)
    acc3 = carve(xar, 0, F16, KC, 1024)
    hidt = [carve(xar, 32768, F16, IC, 1024), carve(r2, 0, F16, IC, 1024)]
    sgt = carve(xar, 55296, F16, 1024)
    cbct = carve(xar, 57344, F16, 1024)
    sbct = carve(xar, 59392, F16, 1024)
    h23 = carve(r1, 0, F16, KC, S)

    def sb(name, shape, dtype):
        return Buf(nc.alloc_sbuf_tensor(name, shape, dtype).ap(), name)

    xB = [Buf(x3, "x3h0"), Buf(x3, "x3h1")]
    h1B = Buf(h1t, "h1t")
    qB = Buf(qt, "qt")
    kvB = Buf(kv3, "kv3")
    sqB = [Buf(kv3[:, :, 0:512], "sq0"), Buf(kv3[:, :, 512:1024], "sq1")]
    accB = Buf(acc3, "acc3")
    hidB = [Buf(hidt[0], "hid0"), Buf(hidt[1], "hid1")]
    sgB = [Buf(sgt[:, 0:512], "sgt0"), Buf(sgt[:, 512:1024], "sgt1")]
    cbB = Buf(cbct, "cbc")
    sbB = Buf(sbct, "sbc")
    h2B = [Buf(h23, "h23a"), Buf(h23, "h23b")]

    consB = sb("cons", [128, 256], F16)
    ident = consB.t[:, 0:128]
    ones1 = consB.t[:, 128:256]
    crawB = sb("c_raw", [128, 16], F32)
    silucB = sb("silu_c", [128, 16], F16)
    modvB = sb("modv", [128, 96], F32)
    modv = modvB.t
    carryB = sb("carry", [128, 16], F32)
    lnAB = sb("lnA", [128, 512], F16)
    lnBB = sb("lnB", [128, 512], F16)
    lnTB = [sb("lnT0", [128, 512], F16), sb("lnT1", [128, 512], F16)]
    lnrow_t = nc.alloc_sbuf_tensor("lnrow", [1, 2, 512], F16).ap()
    lnrB = [Buf(lnrow_t[:, 0, :], "lnr0"), Buf(lnrow_t[:, 1, :], "lnr1")]
    lnFB = sb("lnF", [128, 512], F32)
    stg_t = nc.alloc_sbuf_tensor("stg", [128, 2, 512], F16).ap()
    stg2 = stg_t[:, 0:2, :]
    stgB = [Buf(stg_t[:, 0, :], "stg0"), Buf(stg_t[:, 1, :], "stg1")]
    mrowB = sb("mrow", [1, 512], F16)
    lgB = sb("lg", [8, 512], F16)
    rtB = sb("rt8", [128, 8], F32)
    rpB = sb("rp8", [128, 8], F32)
    rmB = sb("rm8", [128, 8], F32)
    rkB = sb("rk8", [128, 8], F32)
    cmbB = sb("cmb16", [128, 8], F16)
    m1B = sb("m1c", [128, 1], F32)
    m2B = sb("m2c", [128, 1], F32)
    negmB = sb("negm", [128, 1], F32)
    denB = sb("denc", [128, 1], F32)
    crowB = sb("crow", [8, 1024], F16)
    srowB = sb("srow", [1, 1024], F16)
    selB = sb("sel", [8, 1024], F16)
    rtwB = sb("rt_w", [128, 128], F16)
    sgwB = sb("sg_w", [128, 16], F16)
    x1rB = [sb("x1r0", [128, 1024], F16), sb("x1r1", [128, 1024], F16)]
    orB = [sb(f"oring{i}", [128, 512], F32) for i in range(2)]
    tfinB = sb("tfin", [128, 512], F32)

    psb = [Buf(stack.enter_context(
        nc.psum_tensor(f"ps{i}", [128, 512], F32)).ap(), f"ps{i}")
        for i in range(7)]
    pstr_t = stack.enter_context(nc.psum_tensor("pstr", [128, 1024], F16)).ap()
    pstrA = Buf(pstr_t[:, 0:512], "pstrA")
    pstrB = pstrA

    pk = PK(nc, stack)
    pool = BankPool(psb)
    ring = Ring(pk, [wring_t[:, s * 4096:(s + 1) * 4096].bitcast(F16)
                     for s in range(4)])

    pk.dma("sy", consB.t, CONSTS, writes=[consB])
    pk.dma("sy", crawB.t, CC, writes=[crawB])
    pk.dma("sy", selB.t, SEL, writes=[selB])
    pk.dma("gp", rtwB.t, RT, writes=[rtwB])
    pk.dma("gp", sgwB.t, SGW, writes=[sgwB])
    pk.a("activation", silucB.t, crawB.t, AF.Silu,
         reads=[crawB], writes=[silucB])
    pk.v("memset", carryB.t, 0.0, writes=[carryB])

    def mod_mm_unit(fs, k, bA, bB):
        p = ring.load(WMOD[fs, k], 1, 1024)
        for h, bank in ((0, bA), (1, bB)):
            pk.pe_mm(bank, silucB.t[:, k:k + 1],
                     p["view"][:, 0, h * 512:(h + 1) * 512],
                     start=(k == 0), stop=(k == KC - 1),
                     rbufs=[p["buf"], silucB],
                     out=bank.t[0:1, :], inc=(h == 1))

    def mod_extract(fs, bA, bB):
        v = fs // 2
        cbase = (fs % 2) * 8
        for h, bank in ((0, bA), (1, bB)):
            pk.a("copy", mrowB.t, bank.t[0:1, :], reads=[bank],
                 writes=[mrowB])
            for j in range(4):
                pk.pe_tr(pstrA, pstrA.t[:, 0:1], mrowB,
                         mrowB.t[0:1, j * 128:(j + 1) * 128],
                         consB, ident[0:1, 0:1])
                col = v * 16 + cbase + h * 4 + j
                pk.a("copy", modv[:, col:col + 1], pstrA.t[:, 0:1],
                     reads=[pstrA], writes=[modvB])

    def mod_unit_list(fs_range, bA, bB):
        units = []
        for fs in fs_range:
            for k in range(KC):
                units.append(lambda fs=fs, k=k: mod_mm_unit(fs, k, bA, bB))
            units.append(lambda fs=fs: mod_extract(fs, bA, bB))
        return units

    def emit_mod(fs_range):
        bA, bB = pool.take(), pool.take()
        for u in mod_unit_list(fs_range, bA, bB):
            u()

    with nc.named_scope("modA"):
        emit_mod(range(2))
        pk.dma("gp", x3, XIN, writes=xB)
        emit_mod(range(2, 4))
    if upto == "modraw":
        emit_mod(range(4, 12))

    def emit_gates(vs):
        for vname in vs:
            c0 = MODC[vname]
            sl = modv[:, c0:c0 + 16]
            if vname in ("ca", "cb"):
                pk.v("tensor_single_scalar", sl, sl, 1.0, ALU.add,
                     reads=[modvB], writes=[modvB])
            else:
                sgn = lnFB.t[:, 0:16]
                mag = lnFB.t[:, 16:32]
                pk.v("tensor_single_scalar", sgn, sl, 0.0, ALU.is_ge,
                     reads=[modvB], writes=[lnFB])
                pk.v("tensor_scalar", sgn, sgn, 2.0, -1.0,
                     op0=ALU.mult, op1=ALU.add,
                     reads=[lnFB], writes=[lnFB])
                pk.a("activation", mag, sl, AF.Abs,
                     reads=[modvB], writes=[lnFB])
                pk.v("tensor_single_scalar", mag, mag, 0.25, ALU.max,
                     reads=[lnFB], writes=[lnFB])
                pk.v("tensor_mul", sl, sgn, mag,
                     reads=[lnFB], writes=[modvB])

    if upto != "modraw":
        emit_gates(["ca"])

    def emit_ln(srcB, src3, dstB, dst3, ca_col, sa_col, s_lo, s_n, d_lo=None):
        if d_lo is None:
            d_lo = s_lo
        subs = list(range(s_n // 512))
        state = {}

        def squares(sub):
            s0 = s_lo + sub * 512
            sB_half = srcB[s0 // 1024] if isinstance(srcB, list) else srcB
            sq = sqB[sub % 2]
            for k in range(0, KC, 2):
                xs2 = src3[:, k:k + 2, s0:s0 + 512]
                pk.v("tensor_mul", sq.t[:, k:k + 2, :], xs2, xs2,
                     reads=[sB_half], writes=[sq])

        def stats(sub):
            s0 = s_lo + sub * 512
            sB_half = srcB[s0 // 1024] if isinstance(srcB, list) else srcB
            sq = sqB[sub % 2]
            bA, bB = pool.take(), pool.take()
            for k in range(KC):
                xs = src3[:, k, s0:s0 + 512]
                pk.pe_mm(bA, ones1, xs, start=(k == 0),
                         stop=(k == KC - 1), rbufs=[consB, sB_half])
                pk.pe_mm(bB, ones1, sq.t[:, k, :], start=(k == 0),
                         stop=(k == KC - 1), rbufs=[consB, sq])
            state[sub] = (bA, bB, s0, s_lo and 0)

        def tail(sub):
            bA, bB, s0, _ = state.pop(sub)
            d0 = d_lo + sub * 512
            sB_half = srcB[s0 // 1024] if isinstance(srcB, list) else srcB
            dB_half = dstB[(d_lo + sub * 512) // 1024] \
                if isinstance(dstB, list) else dstB
            r0, r1 = lnrB
            r2 = mrowB
            r3s = srowB.t[0:1, 0:512]
            pk.a("activation", r0.t, bA.t[0:1, :], AF.Identity,
                 scale=1.0 / 2048.0, reads=[bA], writes=[r0])
            pk.a("activation", r1.t, bB.t[0:1, :], AF.Identity,
                 scale=1.0 / 2048.0, reads=[bB], writes=[r1])
            pk.v("tensor_mul", r2.t, r0.t, r0.t, reads=[r0], writes=[mrowB])
            pk.v("tensor_tensor", r1.t, r1.t, r2.t, op=ALU.subtract,
                 reads=[r1, mrowB], writes=[r1])
            pk.v("tensor_single_scalar", r1.t, r1.t, EPS, ALU.add,
                 reads=[r1], writes=[r1])
            with nc.allow_low_precision(reason="f16 LN row scratch, 5e-4 rel"):
                pk.v("reciprocal", r2.t, r1.t, reads=[r1], writes=[mrowB])
            pk.a("sqrt", r3s, r2.t, reads=[mrowB], writes=[srowB])
            pk.v("scalar_tensor_tensor", r1.t, r0.t, -1.0, r3s,
                 op0=ALU.mult, op1=ALU.mult,
                 reads=[r0, srowB], writes=[r1])
            bc1, bc2 = pool.take(), pool.take()
            pk.pe_mm(bc1, ones1[0:1, :], r3s, start=True, stop=True,
                     rbufs=[consB, srowB])
            pk.pe_mm(bc2, ones1[0:1, :], r1.t, start=True, stop=True,
                     rbufs=[consB, r1])
            pk.v("tensor_copy", lnAB.t, bc1.t, reads=[bc1], writes=[lnAB])
            pk.v("tensor_copy", lnBB.t, bc2.t, reads=[bc2], writes=[lnBB])
            for k in range(KC):
                tB = lnTB[k % 2]
                pk.v("tensor_mul", tB.t, src3[:, k, s0:s0 + 512], lnAB.t,
                     reads=[sB_half, lnAB], writes=[tB])
                pk.v("tensor_tensor", tB.t, tB.t, lnBB.t, op=ALU.add,
                     reads=[tB, lnBB], writes=[tB])
                pk.a("activation", dst3[:, k, d0:d0 + 512], tB.t,
                     AF.Identity,
                     bias=modv[:, sa_col + k:sa_col + k + 1],
                     scale=modv[:, ca_col + k:ca_col + k + 1],
                     reads=[tB, modvB], writes=[dB_half])

        squares(subs[0])
        for sub in subs:
            stats(sub)
            if sub + 1 in subs:
                squares(sub + 1)
            if sub - 1 in subs:
                tail(sub - 1)
        tail(subs[-1])

    def emit_qkv(t, mod_units=()):
        mod_units = list(mod_units)
        P0, P1, P2, P3 = psb[0], psb[1], psb[2], psb[3]
        nu = (len(mod_units) + 15) // 16
        for f in range(16):
            pq = ring.load(WQ[f], KC, 128)
            pkk = ring.load(WK[f], KC, 128)
            for p, bb in ((pq, (P0, P1)), (pkk, (P2, P3))):
                for st in range(2):
                    for k in range(KC):
                        pk.pe_mm(bb[st], p["view"][:, k, :],
                                 h1t[:, k, st * 512:(st + 1) * 512],
                                 start=(k == 0), stop=(k == KC - 1),
                                 rbufs=[p["buf"], h1B])
            for st, b in ((0, P0), (1, P1)):
                pk.a("copy", qt[:, f, st * 512:(st + 1) * 512], b.t,
                     reads=[b], writes=[qB])
            for st, b in ((0, P2), (1, P3)):
                pk.a("copy", stgB[st].t, b.t, reads=[b], writes=[stgB[st]])
            for u in mod_units[f * nu:(f + 1) * nu]:
                u()
            pv = ring.load(WV[f], KC, 128)
            for st, b in ((0, P2), (1, P3)):
                for k in range(KC):
                    pk.pe_mm(b, pv["view"][:, k, :],
                             h1t[:, k, st * 512:(st + 1) * 512],
                             start=(k == 0), stop=(k == KC - 1),
                             rbufs=[pv["buf"], h1B])
                pk.v("tensor_mul", kv3[:, f, st * 512:(st + 1) * 512],
                     stgB[st].t, b.t,
                     reads=[stgB[st], b], writes=[kvB])
            k = f
            pk.v("tensor_tensor_scan", kv3[:, k, :], kv3[:, k, :],
                 ones1[:, 0:1].broadcast_to([128, 1024]),
                 carryB.t[:, k:k + 1], op0=ALU.add, op1=ALU.mult,
                 reads=[kvB, consB, carryB], writes=[kvB])
            pk.v("tensor_copy", carryB.t[:, k:k + 1], kv3[:, k, 1023:1024],
                 reads=[kvB], writes=[carryB])
            pk.v("scalar_tensor_tensor", qt[:, k, :], qt[:, k, :],
                 DENOM, kv3[:, k, :], op0=ALU.mult, op1=ALU.mult,
                 reads=[qB, kvB], writes=[qB])

    def emit_wo(t):
        s0 = t * 1024
        for f in range(16):
            po = ring.load(WO[f], KC, 128)
            for st in range(2):
                b = pool.take()
                for k in range(KC):
                    pk.pe_mm(b, po["view"][:, k, :],
                             qt[:, k, st * 512:(st + 1) * 512],
                             start=(k == 0), stop=(k == KC - 1),
                             rbufs=[po["buf"], qB])
                ga_c = MODC["ga"] + f
                xs = x3[:, f, s0 + st * 512:s0 + (st + 1) * 512]
                pk.v("scalar_tensor_tensor", xs, b.t,
                     modv[:, ga_c:ga_c + 1], xs,
                     op0=ALU.mult, op1=ALU.add,
                     reads=[b, modvB], writes=[xB[t]])

    m_spill = None
    if upto in ("attn", "ln2", "full"):
        with nc.named_scope("ln1a"):
            pk.alias(sqB, [kvB])
            emit_ln(xB, x3, h1B, h1t, MODC["ca"], MODC["sa"], 0, 1024)
        pk.alias([kvB], sqB)
        with nc.named_scope("qkv0"):
            emit_qkv(0, mod_unit_list(range(4, 6), psb[4], psb[5]))
            emit_gates(["ga"])
        with nc.named_scope("ln1b"):
            pk.alias(sqB, [kvB])
            emit_ln(xB, x3, h1B, h1t, MODC["ca"], MODC["sa"], 1024, 1024,
                    d_lo=0)
        pk.alias([kvB], sqB)
        with nc.named_scope("wo0"):
            emit_wo(0)
        with nc.named_scope("qkv1"):
            emit_qkv(1, mod_unit_list(range(6, 12), psb[4], psb[5]))
            emit_gates(["cb", "gb"])
        with nc.named_scope("wo1"):
            emit_wo(1)

    if upto in ("ln2", "full"):
        pk.alias(h2B, [h1B, qB])
        with nc.named_scope("ln2"):
            pk.alias(sqB, [kvB])
            emit_ln(xB, x3, h2B, h23, MODC["cb"], MODC["sb"], 0, 2048)
        m_spill = pk.dma("sy", XSP, x3, reads=xB)
        pk.alias([accB, hidB[0], sgB[0], sgB[1], cbB, sbB], xB)
        pk.alias([hidB[1]], [kvB] + sqB)

    def gu_src(pe_i, j):
        if pe_i < 4:
            return SHGU[22 * pe_i + j]
        return GUP[pe_i - 4, j]

    def dn_src(pe_i, f):
        if pe_i < 4:
            return SHD[f, pe_i]
        return DWN[pe_i - 4, f]

    def emit_ffn_half(hf):
        s0 = hf * 1024
        for st in range(2 if ffn_lim >= -3 else 0):
            b = pool.take()
            for k in range(KC):
                pk.pe_mm(b, rtwB.t[:, k * 8:(k + 1) * 8],
                         h23[:, k, s0 + st * 512:s0 + (st + 1) * 512],
                         start=(k == 0), stop=(k == KC - 1),
                         rbufs=[rtwB, h2B[hf]], out=b.t[0:8, :])
            pk.a("copy", lgB.t, b.t[0:8, :], reads=[b], writes=[lgB])
            for j in range(4 if ffn_lim >= -2 else 0):
                pk.pe_tr(pstrA, pstrA.t[0:128, 0:8], lgB,
                         lgB.t[0:8, j * 128:(j + 1) * 128],
                         consB, ident[0:8, 0:8])
                pk.v("tensor_copy", rtB.t, pstrA.t[0:128, 0:8],
                     reads=[pstrA], writes=[rtB])
                pk.v("reduce_max", m1B.t, rtB.t, axis=AX,
                     reads=[rtB], writes=[m1B])
                pk.v("tensor_single_scalar", negmB.t, m1B.t, -1.0,
                     ALU.mult, reads=[m1B], writes=[negmB])
                pk.v("tensor_single_scalar", rkB.t, rtB.t, m1B.t,
                     ALU.is_ge, reads=[rtB, m1B], writes=[rkB])
                pk.v("scalar_tensor_tensor", rpB.t, rkB.t, -1e30, rtB.t,
                     op0=ALU.mult, op1=ALU.add,
                     reads=[rkB, rtB], writes=[rpB])
                pk.v("reduce_max", m2B.t, rpB.t, axis=AX,
                     reads=[rpB], writes=[m2B])
                pk.a("activation", rpB.t, rtB.t, AF.Exp, bias=negmB.t,
                     reads=[rtB, negmB], writes=[rpB])
                pk.v("tensor_single_scalar", rkB.t, rtB.t, m2B.t,
                     ALU.is_ge, reads=[rtB, m2B], writes=[rkB])
                pk.v("tensor_mul", rmB.t, rpB.t, rkB.t,
                     reads=[rpB, rkB], writes=[rmB])
                pk.v("reduce_sum", denB.t, rmB.t, axis=AX,
                     reads=[rmB], writes=[denB])
                pk.v("reciprocal", denB.t, denB.t,
                     reads=[denB], writes=[denB])
                pk.v("tensor_single_scalar", cmbB.t, rmB.t, denB.t,
                     ALU.mult, reads=[rmB, denB], writes=[cmbB])
                if ffn_lim >= -1:
                    pk.pe_tr(pstrB, pstr_t[0:8, 512:640], cmbB, cmbB.t,
                             consB, ident)
                    pk.a("copy",
                         crowB.t[0:8, st * 512 + j * 128:st * 512 + (j + 1) * 128],
                         pstr_t[0:8, 512:640], reads=[pstrB], writes=[crowB])
        for st in range(2 if ffn_lim >= 0 else 0):
            b = pool.take()
            for k in range(KC):
                pk.pe_mm(b, sgwB.t[:, k:k + 1],
                         h23[:, k, s0 + st * 512:s0 + (st + 1) * 512],
                         start=(k == 0), stop=(k == KC - 1),
                         rbufs=[sgwB, h2B[hf]], out=b.t[0:1, :])
            pk.a("activation", srowB.t[0:1, st * 512:(st + 1) * 512],
                 b.t[0:1, :], AF.Sigmoid, reads=[b], writes=[srowB])
            b2 = pool.take()
            pk.pe_mm(b2, ones1[0:1, :],
                     srowB.t[0:1, st * 512:(st + 1) * 512],
                     start=True, stop=True, rbufs=[consB, srowB])
            pk.v("tensor_copy", sbct[:, st * 512:(st + 1) * 512], b2.t,
                 reads=[b2], writes=[sbB])
        for pe_i in range(min(NPE, ffn_lim)):
            scope = stack.enter_context(nc.named_scope(f"pe{hf}_{pe_i}")) if False else None
            slot = pe_i % 2
            if pe_i < 4:
                scB, sct = sbB, sbct
            else:
                e = pe_i - 4
                for st in range(2):
                    b2 = pool.take()
                    pk.pe_mm(b2, selB.t[:, e * 128:(e + 1) * 128],
                             crowB.t[0:8, st * 512:(st + 1) * 512],
                             start=True, stop=True, rbufs=[selB, crowB])
                    pk.v("tensor_copy", cbct[:, st * 512:(st + 1) * 512],
                         b2.t, reads=[b2], writes=[cbB])
                scB, sct = cbB, cbct
            gu_scope = nc.named_scope(f"gu{hf}_{pe_i}")
            gu_scope.__enter__()
            for i in range(IC):
                pg = ring.load(gu_src(pe_i, 2 * i), KC, 128)
                pu = ring.load(gu_src(pe_i, 2 * i + 1), KC, 128)
                bg = [pool.take(), pool.take()]
                bu = [pool.take(), pool.take()]
                for p, bb in ((pg, bg), (pu, bu)):
                    for st in range(2):
                        for k in range(KC):
                            pk.pe_mm(bb[st], p["view"][:, k, :],
                                     h23[:, k, s0 + st * 512:s0 + (st + 1) * 512],
                                     start=(k == 0), stop=(k == KC - 1),
                                     rbufs=[p["buf"], h2B[hf]])
                for st in range(2):
                    pk.a("activation", sgt[:, st * 512:(st + 1) * 512],
                         bg[st].t, AF.Silu, reads=[bg[st]], writes=[sgB[st]])
                    hv = hidt[slot][:, i, st * 512:(st + 1) * 512]
                    pk.v("tensor_mul", hv, bu[st].t,
                         sct[:, st * 512:(st + 1) * 512],
                         reads=[bu[st], scB], writes=[hidB[slot]])
                    pk.v("tensor_mul", hv, hv,
                         sgt[:, st * 512:(st + 1) * 512],
                         reads=[hidB[slot], sgB[st]], writes=[hidB[slot]])
            gu_scope.__exit__(None, None, None)
            if pe_i % 2 == 1:
                r = pe_i // 2
                peA, peB = pe_i - 1, pe_i
                dn_scope = nc.named_scope(f"dn{hf}_{r}")
                dn_scope.__enter__()
                for f in range(16):
                    pa = ring.load(dn_src(peA, f), IC, 128)
                    pb = ring.load(dn_src(peB, f), IC, 128)
                    if r == 5:
                        pk.dma("sy", x1rB[f % 2].t,
                               XSP[:, f, s0:s0 + 1024],
                               marks=[m_spill], writes=[x1rB[f % 2]])
                    for st in range(2):
                        b = pool.take()
                        for k in range(IC):
                            pk.pe_mm(b, pa["view"][:, k, :],
                                     hidt[peA % 2][:, k, st * 512:(st + 1) * 512],
                                     start=(k == 0), stop=False,
                                     rbufs=[pa["buf"], hidB[peA % 2]])
                        for k in range(IC):
                            pk.pe_mm(b, pb["view"][:, k, :],
                                     hidt[peB % 2][:, k, st * 512:(st + 1) * 512],
                                     start=False, stop=(k == IC - 1),
                                     rbufs=[pb["buf"], hidB[peB % 2]])
                        a_sl = acc3[:, f, st * 512:(st + 1) * 512]
                        if r == 0:
                            pk.a("copy", a_sl, b.t, reads=[b], writes=[accB])
                        elif r < 5:
                            pk.v("tensor_tensor", a_sl, b.t, a_sl,
                                 op=ALU.add, reads=[b, accB], writes=[accB])
                        else:
                            gb_c = MODC["gb"] + f
                            pk.v("tensor_tensor", tfinB.t, b.t, a_sl,
                                 op=ALU.add, reads=[b, accB], writes=[tfinB])
                            sidx = (f * 2 + st) % 2
                            pk.v("scalar_tensor_tensor", orB[sidx].t,
                                 tfinB.t, modv[:, gb_c:gb_c + 1],
                                 x1rB[f % 2].t[:, st * 512:(st + 1) * 512],
                                 op0=ALU.mult, op1=ALU.add,
                                 reads=[tfinB, modvB, x1rB[f % 2]],
                                 writes=[orB[sidx]])
                            pk.dma("sy",
                                   OUT[:, f, s0 + st * 512:s0 + (st + 1) * 512],
                                   orB[sidx].t, reads=[orB[sidx]])
                _dn_last = (f == 15)
                if _dn_last:
                    dn_scope.__exit__(None, None, None)

    if upto == "full":
        for hf in range(2 if ffn_lim >= 99 else 1):
            with nc.named_scope(f"ffn{hf}"):
                emit_ffn_half(hf)

        if ffn_lim >= 99:
            pk.final_dma_wait()
        else:
            zm = pk.v("memset", lnTB[0].t, 0.0,
                      marks=[("dve", pk.cnt["dve"]), ("act", pk.cnt["act"]),
                             ("pe", pk.cnt["pe"])], writes=[lnTB[0]])
            for kk in range(KC):
                for st4 in range(4):
                    pk.dma("sy", OUT[:, kk, st4 * 512:(st4 + 1) * 512],
                           lnTB[0].t, reads=[lnTB[0]])
            pk.final_dma_wait()

    if dump is not None:
        allm = [("dve", pk.cnt["dve"]), ("act", pk.cnt["act"]),
                ("pe", pk.cnt["pe"])]
        if dump == "mod":
            pk.dma("sy", DBG[:, 0, 0:96], modv, marks=allm)
        elif dump == "x1":
            pk.dma("gp", DBG, x3, marks=allm)
        elif dump == "h1":
            pk.dma("gp", DBG[:, :, 0:1024], h1t, marks=allm)
        elif dump == "kv":
            pk.dma("gp", DBG[:, :, 0:1024], kv3, marks=allm)
        elif dump == "h2":
            pk.dma("gp", DBG, h23, marks=allm)
        pk.final_dma_wait()
        if upto != "full":
            zm = pk.v("memset", lnTB[0].t, 0.0, marks=allm,
                      writes=[lnTB[0]])
            for kk in range(KC):
                for st4 in range(4):
                    pk.dma("sy", OUT[:, kk, st4 * 512:(st4 + 1) * 512],
                           lnTB[0].t, reads=[lnTB[0]])
            pk.final_dma_wait()

    stack.close()
    return nc


_NC_CACHE = {}


def _get_nc(upto="full", dump=None, ffn_lim=99):
    key = (upto, dump, ffn_lim)
    if key not in _NC_CACHE:
        _NC_CACHE[key] = _build(upto, dump, ffn_lim)
    return _NC_CACHE[key]



def _pan2048(W):
    A = np.ascontiguousarray(np.asarray(W, dtype=np.float32).T)
    return np.ascontiguousarray(
        A.reshape(16, 128, 16, 128).transpose(2, 1, 0, 3))


def _prep_inputs(ins):
    x = np.asarray(ins["L0_attn_x"], dtype=np.float32)
    c = np.asarray(ins["L0_c"], dtype=np.float32)
    wmod = np.asarray(ins["L0_wmod"], dtype=np.float32)
    gup = np.asarray(ins["L0_ffn_gate_up_proj"], dtype=np.float32)
    dwn = np.asarray(ins["L0_ffn_down_proj"], dtype=np.float32)
    shg = np.asarray(ins["L0_ffn_sh_gate_w"], dtype=np.float32)
    shu = np.asarray(ins["L0_ffn_sh_up_w"], dtype=np.float32)
    shd = np.asarray(ins["L0_ffn_sh_down_w"], dtype=np.float32)
    rw = np.asarray(ins["L0_ffn_router_weight"], dtype=np.float32)
    sgw = np.asarray(ins["L0_ffn_shared_gate_w"], dtype=np.float32)

    shared = {
        "WQ": _pan2048(ins["L0_attn_Wq"]),
        "WK": _pan2048(ins["L0_attn_Wk"]),
        "WV": _pan2048(ins["L0_attn_Wv"]),
        "WO": _pan2048(ins["L0_attn_Wo"]),
        "WMOD": np.ascontiguousarray(
            wmod.reshape(16, 128, 12, 1024).transpose(2, 0, 1, 3)),
        "RT": np.ascontiguousarray(
            rw.T.reshape(16, 128, 8).transpose(1, 0, 2).reshape(128, 128)),
        "SGW": np.ascontiguousarray(sgw[0].reshape(16, 128).T),
        "CONSTS": np.concatenate(
            [np.eye(128, dtype=np.float32),
             np.ones((128, 128), np.float32)], axis=1).astype(np.float16),
        "SEL": np.repeat(np.eye(8, dtype=np.float32), 128,
                         axis=1).astype(np.float16),
    }
    idx = []
    for i in range(11):
        idx += [i, 11 + i]
    g8 = []
    for e in range(8):
        A = np.ascontiguousarray(gup[e].T)
        g8.append(A.reshape(16, 128, 22, 128).transpose(2, 1, 0, 3)[idx])
    shared["GUP"] = np.ascontiguousarray(np.stack(g8))
    d8 = []
    for e in range(8):
        A = np.ascontiguousarray(dwn[e].T)
        d8.append(A.reshape(11, 128, 16, 128).transpose(2, 1, 0, 3))
    shared["DWN"] = np.ascontiguousarray(np.stack(d8))
    Hg = np.ascontiguousarray(shg.T).reshape(
        16, 128, 44, 128).transpose(2, 1, 0, 3)
    Hu = np.ascontiguousarray(shu.T).reshape(
        16, 128, 44, 128).transpose(2, 1, 0, 3)
    Hgu = np.empty((88,) + Hg.shape[1:], dtype=np.float32)
    Hgu[0::2] = Hg
    Hgu[1::2] = Hu
    shared["SHGU"] = np.ascontiguousarray(Hgu)
    Ad = np.ascontiguousarray(shd.T)
    shared["SHD"] = np.ascontiguousarray(
        Ad.reshape(4, 11, 128, 16, 128).transpose(3, 0, 2, 1, 4))

    in_maps = []
    for b in range(8):
        m = dict(shared)
        m["XIN"] = np.ascontiguousarray(
            x[b].T.reshape(16, 128, 2048).transpose(1, 0, 2))
        m["CC"] = np.ascontiguousarray(c[b].reshape(16, 128).T)
        in_maps.append(m)
    return in_maps


def kernel(**ins):
    in_maps = _prep_inputs(ins)
    nc = _get_nc()
    res = run_bass_kernel_spmd(nc, in_maps, core_ids=list(range(8)))
    outs = []
    for b in range(8):
        o = np.asarray(res.results[b]["OUT"])
        outs.append(o.transpose(1, 0, 2).reshape(2048, 2048).T)
    return np.ascontiguousarray(np.stack(outs)).astype(np.float32)
